# version 04
# baseline (speedup 1.0000x reference)
import math
import numpy as np
from contextlib import ExitStack

import concourse.bass as bass
import concourse.mybir as mybir
from concourse.bass_utils import run_bass_kernel_spmd

F32 = mybir.dt.float32
BF16 = mybir.dt.bfloat16
AF = mybir.ActivationFunctionType
ALU = mybir.AluOpType
AX = mybir.AxisListType

T_CORE = 2048
NTT = 16
ND = 8
D = 1024
BIG = 1.0e30
ISQ2 = 0.7071067811865476


class Ch:
    def __init__(self, nc, st, name):
        self.h = st.enter_context(nc.semaphore(name))
        self.n = 0

    def inc(self, instr, k=1):
        instr.then_inc(self.h, k)
        self.n += k
        return self.n


def _w(eng, *tok):
    for ch, v in tok:
        if v > 0:
            eng.wait_ge(ch.h, v)



def build_A():
    nc = bass.Bass("TRN2", target_bir_lowering=False)
    dt = nc.dram_tensor
    x_h = dt("x", [T_CORE, D], F32, kind="ExternalInput")
    wq0_h = dt("wq0T", [D, D], F32, kind="ExternalInput")
    wv0_h = dt("wv0T", [D, D], F32, kind="ExternalInput")
    bq0_h = dt("bq0", [128, 8], F32, kind="ExternalInput")
    bv0_h = dt("bv0", [128, 8], F32, kind="ExternalInput")
    wvq_h = dt("wavq", [128, 24], F32, kind="ExternalInput")
    wvv_h = dt("wavv", [128, 24], F32, kind="ExternalInput")
    wo_h = dt("woT", [D, D], F32, kind="ExternalInput")
    bo_h = dt("bo", [1, D], F32, kind="ExternalInput")
    gw_h = dt("gateT", [D, 16], F32, kind="ExternalInput")
    be_h = dt("biase", [128, 16], F32, kind="ExternalInput")
    up_h = dt("upT", [16, D, 512], F32, kind="ExternalInput")
    dn_h = dt("dnT", [16, 512, D], F32, kind="ExternalInput")
    sup_h = dt("supT", [D, 2048], F32, kind="ExternalInput")
    sdn_h = dt("sdnT", [2048, D], F32, kind="ExternalInput")
    w1_h = dt("w1T", [D, 3072], F32, kind="ExternalInput")
    b1qk_h = dt("b1qk", [128, 16], F32, kind="ExternalInput")
    b1v_h = dt("b1v", [1, D], F32, kind="ExternalInput")
    idb_h = dt("idb", [128, 128], F32, kind="ExternalInput")
    onesr_h = dt("onesr", [1, 128], F32, kind="ExternalInput")
    ones4_h = dt("ones4", [128, 4], F32, kind="ExternalInput")
    wsel_h = dt("wsel", [128, 16], F32, kind="ExternalInput")
    out1_h = dt("out1", [T_CORE, D], F32, kind="ExternalOutput")
    xn1_h = dt("xn1T", [D, T_CORE], BF16, kind="ExternalOutput")
    qs_h = dt("qsT", [D, T_CORE], BF16, kind="ExternalOutput")
    kt_h = dt("kT", [D, T_CORE], BF16, kind="ExternalOutput")
    vo_h = dt("vN", [T_CORE, D], BF16, kind="ExternalOutput")

    st = ExitStack()
    with st:
        sb = lambda n, s, d_: st.enter_context(nc.sbuf_tensor(n, s, d_))
        pst = lambda n, s, d_: st.enter_context(nc.psum_tensor(n, s, d_))
        ACC = sb("ACC", [128, NTT * D], F32)
        XNT = sb("XNT", [128, ND * T_CORE], BF16)
        AT = sb("AT", [128, ND * T_CORE], BF16)
        HF = sb("HF", [128, 6144], F32)
        SCR = sb("SCR", [128, 4096], F32)
        WU = [sb(f"WU{i}", [128, 8 * 512], BF16) for i in range(2)]
        WBIG = sb("WBIG", [128, 8 * D], BF16)
        WD = [WBIG[:, 0:4096], WBIG[:, 4096:8192]]
        WQS = [WU[0][:, 0:1024], WU[1][:, 0:1024]]
        WVS = [WU[0][:, 1024:2048], WU[1][:, 1024:2048]]
        WO = WBIG[:, 0:8192]
        W1V = WBIG[:, 0:8192]
        XNB0 = sb("XNB0", [128, D], BF16)
        XNB = [XNB0, XNB0]
        STG = [AT[:, 0:2048], AT[:, 2048:4096]]
        HBOF = 4096
        BQ0 = sb("BQ0", [128, 8], F32)
        BV0 = sb("BV0", [128, 8], F32)
        WAQ = sb("WAQ", [128, 24], F32)
        WAV = sb("WAV", [128, 24], F32)
        WQE = sb("WQE", [128, 24], F32)
        BO = SCR[0:1, 1024:2048]
        B1V = BO
        GW = sb("GW", [128, 8 * 16], BF16)
        BE = sb("BE", [128, 16], F32)
        B1QK = sb("B1QK", [128, 16], F32)
        B1Q32 = sb("B1Q32", [128, 8], F32)
        EPSB = sb("EPSB", [128, 1], F32)
        IDB = sb("IDB", [128, 128], BF16)
        ONR = sb("ONR", [1, 128], F32)
        ON4 = sb("ON4", [128, 4], F32)
        WSEL = sb("WSEL", [128, 16], F32)
        SCALL = sb("SCALL", [128, NTT * 16], F32)
        SC2 = sb("SC2", [128, 16], F32)
        GS = sb("GS", [128, 4], F32)
        GM = sb("GM", [128, 4], F32)
        SMASK = sb("SMASK", [128, 16], F32)
        SEL = sb("SEL", [128, 16], F32)
        RSC = sb("RSC", [128, 8], F32)
        COMB = sb("COMB", [128, NTT * 16], F32)
        SQT = SCR[:, 0:512]
        P = [pst(f"P{i}", [128, 512], F32) for i in range(4)]
        PT = [pst(f"PTp{i}", [128, 128], BF16) for i in range(2)]

        cg = Ch(nc, st, "cg")
        cy = Ch(nc, st, "cy")
        ct = Ch(nc, st, "ct")
        cv = Ch(nc, st, "cv")
        cs = Ch(nc, st, "cs")
        co = Ch(nc, st, "co")
        G, Y, T, V, S = nc.gpsimd, nc.sync, nc.tensor, nc.vector, nc.scalar
        pfv = {0: 0, 1: 0, 2: 0, 3: 0}
        pfs = {0: 0, 1: 0, 2: 0, 3: 0}

        for i in range(4):
            cy.inc(Y.dma_start(ACC[:, i * 4 * D:(i + 1) * 4 * D],
                               x_h[i * 512:(i + 1) * 512, :]
                               .rearrange("(n p) d -> p n d", p=128)), 16)
        for dst, srch in ((BQ0, bq0_h), (BV0, bv0_h), (WAQ, wvq_h), (WAV, wvv_h),
                          (BE, be_h), (B1QK, b1qk_h),
                          (ONR, onesr_h), (ON4, ones4_h), (WSEL, wsel_h)):
            cy.inc(Y.dma_start(dst[:], srch[:, :]), 16)
        t_loads = cy.n
        cg.inc(G.dma_start(IDB[:], idb_h[:, :]), 16)
        cg.inc(G.dma_start(GW[:], gw_h[:, :].rearrange("(k p) e -> p k e", p=128)), 16)
        t_gw = cg.n

        _w(V, (cy, t_loads))
        cv.inc(V.tensor_scalar_mul(WQE[:, 0:24:3], WAQ[:, 0:24:3], ISQ2 ** 9))
        cv.inc(V.tensor_scalar_mul(WQE[:, 1:24:3], WAQ[:, 1:24:3], ISQ2 ** 9))
        cv.inc(V.tensor_scalar_mul(WQE[:, 2:24:3], WAQ[:, 2:24:3], ISQ2 ** 6))
        cv.inc(V.tensor_scalar_mul(B1Q32[:], B1QK[:, 0:8], 1.0 / 32.0))
        cv.inc(V.memset(EPSB[:], 1e-5))
        t_prep = cv.n

        def layernorm_and_transpose(dep_v, dep_s):
            _w(S, (cv, dep_v), (cs, dep_s))
            _w(V, (cv, dep_v), (cs, dep_s))
            trans_tok = {}
            for i in range(NTT):
                a = ACC[:, i * D:(i + 1) * D]
                _w(S, (cv, cv.n))
                cs.inc(S.activation(SQT[:], a[:, 0:512], AF.Square))
                _w(V, (cs, cs.n))
                cv.inc(V.reduce_sum(RSC[:, 1:2], SQT[:], axis=AX.X))
                _w(S, (cv, cv.n))
                cs.inc(S.activation(SQT[:], a[:, 512:1024], AF.Square))
                _w(V, (cs, cs.n))
                cv.inc(V.reduce_sum(RSC[:, 2:3], SQT[:], axis=AX.X))
                cv.inc(V.reduce_sum(RSC[:, 0:1], a, axis=AX.X))
                _w(V, (cv, cv.n))
                cv.inc(V.tensor_add(RSC[:, 1:2], RSC[:, 1:2], RSC[:, 2:3]))
                _w(V, (cv, cv.n))
                cv.inc(V.tensor_scalar_mul(RSC[:, 0:1], RSC[:, 0:1], 1.0 / D))
                _w(V, (cv, cv.n))
                cv.inc(V.tensor_scalar_mul(RSC[:, 1:2], RSC[:, 1:2], 1.0 / D))
                cv.inc(V.scalar_tensor_tensor(RSC[:, 3:4], RSC[:, 0:1], -1.0,
                                              RSC[:, 0:1], ALU.mult, ALU.mult))
                _w(V, (cv, cv.n))
                cv.inc(V.tensor_add(RSC[:, 1:2], RSC[:, 1:2], RSC[:, 3:4]))
                _w(S, (cv, cv.n))
                cs.inc(S.activation(RSC[:, 4:5], RSC[:, 1:2], AF.Sqrt,
                                    bias=EPSB[:, 0:1]))
                _w(V, (cs, cs.n))
                cv.inc(V.reciprocal(RSC[:, 5:6], RSC[:, 4:5]))
                _w(V, (cv, cv.n))
                xb = XNB[i % 2]
                if i >= 1:
                    _w(V, (ct, trans_tok[i - 1]))
                t_x = cv.inc(V.tensor_scalar(xb[:], a, RSC[:, 0:1], RSC[:, 5:6],
                                             ALU.subtract, ALU.mult))
                ptfree = getattr(layernorm_and_transpose, "_ptfree", {0: 0, 1: 0})
                for k in range(ND):
                    _w(T, (cv, t_x), (cv, ptfree[k % 2]))
                    ct.inc(T.transpose(PT[k % 2][:], xb[:, k * 128:(k + 1) * 128], IDB[:]))
                    _w(V, (ct, ct.n))
                    ptfree[k % 2] = cv.inc(V.tensor_copy(
                        XNT[:, k * T_CORE + i * 128:k * T_CORE + (i + 1) * 128],
                        PT[k % 2][:]))
                layernorm_and_transpose._ptfree = ptfree
                trans_tok[i] = ct.n
            return cv.n

        t_ln0 = layernorm_and_transpose(t_prep, 0)

        qv_done = {}
        for e in range(ND):
            if e >= 2:
                _w(G, (ct, qv_done[e - 2]))
            else:
                _w(G, (cv, t_ln0))
            cg.inc(G.dma_start(WQS[e % 2],
                               wq0_h[:, e * 128:(e + 1) * 128]
                               .rearrange("(k p) c -> p k c", p=128)), 16)
            cg.inc(G.dma_start(WVS[e % 2],
                               wv0_h[:, e * 128:(e + 1) * 128]
                               .rearrange("(k p) c -> p k c", p=128)), 16)
            t_wl = cg.n
            _w(T, (cg, t_wl), (cv, t_ln0))

            def proj(wbuf, bias, banks):
                for tb in range(4):
                    nb = banks[tb % 2]
                    pb = P[nb]
                    _w(T, (cv, pfv[nb]), (cs, pfs[nb]))
                    for k in range(ND):
                        inst = T.matmul(pb[:], wbuf[:, k * 128:(k + 1) * 128],
                                        XNT[:, k * T_CORE + tb * 512:
                                            k * T_CORE + (tb + 1) * 512],
                                        start=(k == 0), stop=(k == ND - 1))
                    ct.inc(inst)
                    _w(V, (ct, ct.n))
                    pfv[nb] = cv.inc(V.tensor_scalar_add(
                        HF[:, 2048 + tb * 512:2048 + (tb + 1) * 512], pb[:],
                        bias[:, e:e + 1]))

            def wavedec(co):
                _w(V, (cv, cv.n))
                e0, o0 = HF[:, 2048:4096:2], HF[:, 2049:4096:2]
                cv.inc(V.tensor_sub(HF[:, co:co + 1024], e0, o0))
                cv.inc(V.tensor_add(SCR[:, 0:1024], e0, o0))
                _w(V, (cv, cv.n))
                e1, o1 = SCR[:, 0:1024:2], SCR[:, 1:1024:2]
                cv.inc(V.tensor_sub(HF[:, co + 1024:co + 1536], e1, o1))
                cv.inc(V.tensor_add(SCR[:, 1024:1536], e1, o1))
                _w(V, (cv, cv.n))
                e2, o2 = SCR[:, 1024:1536:2], SCR[:, 1025:1536:2]
                cv.inc(V.tensor_sub(HF[:, co + 1536:co + 1792], e2, o2))
                cv.inc(V.tensor_add(HF[:, co + 1792:co + 2048], e2, o2))

            proj(WQS[e % 2], BQ0, (0, 1))
            wavedec(0)
            proj(WVS[e % 2], BV0, (2, 3))
            wavedec(4096)
            cq, cvo = 0, 4096
            _w(V, (cv, cv.n))
            for (qo, vo, j, ln, po) in ((cq + 1792, cvo + 1792, 0, 256, 2048),
                                        (cq + 1536, cvo + 1536, 1, 256, 2304),
                                        (cq + 1024, cvo + 1024, 2, 512, 2560)):
                cv.inc(V.tensor_scalar_mul(SCR[:, 0:ln], HF[:, qo:qo + ln],
                                           WQE[:, e * 3 + j:e * 3 + j + 1]))
                cv.inc(V.tensor_scalar_mul(SCR[:, 1024:1024 + ln], HF[:, vo:vo + ln],
                                           WAV[:, e * 3 + j:e * 3 + j + 1]))
                _w(V, (cv, cv.n))
                cv.inc(V.tensor_mul(SCR[:, po:po + ln], SCR[:, 0:ln],
                                    SCR[:, 1024:1024 + ln]))
                _w(V, (cv, cv.n))
            cv.inc(V.tensor_scalar_mul(SCR[:, 0:1024], HF[:, cq:cq + 1024], ISQ2 ** 3))
            _w(V, (cv, cv.n))
            cv.inc(V.tensor_mul(SCR[:, 3072:4096], SCR[:, 0:1024],
                                HF[:, cvo:cvo + 1024]))
            _w(V, (cv, cv.n))
            cv.inc(V.tensor_add(SCR[:, 0:512:2], SCR[:, 2048:2304], SCR[:, 2304:2560]))
            cv.inc(V.tensor_sub(SCR[:, 1:512:2], SCR[:, 2048:2304], SCR[:, 2304:2560]))
            _w(V, (cv, cv.n))
            cv.inc(V.tensor_add(SCR[:, 512:1536:2], SCR[:, 0:512], SCR[:, 2560:3072]))
            cv.inc(V.tensor_sub(SCR[:, 513:1536:2], SCR[:, 0:512], SCR[:, 2560:3072]))
            _w(V, (cv, cv.n))
            at = AT[:, e * T_CORE:(e + 1) * T_CORE]
            cv.inc(V.tensor_add(at[:, 0:T_CORE:2], SCR[:, 512:1536], SCR[:, 3072:4096]))
            cv.inc(V.tensor_sub(at[:, 1:T_CORE:2], SCR[:, 512:1536], SCR[:, 3072:4096]))
            qv_done[e] = ct.n
        t_at = cv.n

        cg.inc(G.dma_start(WO[:], wo_h[:, :].rearrange("(k p) e -> p k e", p=128)), 16)
        t_wo = cg.n
        _w(Y, (cv, t_at), (cs, cs.n))
        cy.inc(Y.dma_start(BO, bo_h[:, :]), 16)
        t_bo = cy.n

        _w(T, (cg, t_wo), (cv, t_at), (cy, t_bo))
        for i in range(NTT):
            for dh in range(2):
                nb = (2 * i + dh) % 2
                pb = P[nb]
                _w(T, (cv, pfv[nb]), (cs, pfs[nb]))
                for k in range(ND):
                    T.matmul(pb[:], AT[:, k * T_CORE + i * 128:k * T_CORE + (i + 1) * 128],
                             WO[:, k * D + dh * 512:k * D + (dh + 1) * 512],
                             start=(k == 0), stop=False)
                ct.inc(T.matmul(pb[:], ONR[:, :], BO[0:1, dh * 512:(dh + 1) * 512],
                                start=False, stop=True))
                _w(V, (ct, ct.n))
                sl = ACC[:, i * D + dh * 512:i * D + (dh + 1) * 512]
                pfv[nb] = cv.inc(V.tensor_tensor(sl, sl, pb[:], ALU.add))
        t_attn = cv.n

        _w(T, (cg, t_gw))
        scall_tok = {}
        for i in range(NTT):
            _w(T, (cs, pfs[2]))
            for k in range(ND):
                inst = T.matmul(P[2][:, 0:16],
                                XNT[:, k * T_CORE + i * 128:k * T_CORE + (i + 1) * 128],
                                GW[:, k * 16:(k + 1) * 16],
                                start=(k == 0), stop=(k == ND - 1))
            ct.inc(inst)
            _w(S, (ct, ct.n))
            pfs[2] = cs.inc(S.activation(SCALL[:, i * 16:(i + 1) * 16],
                                         P[2][:, 0:16], AF.Sigmoid))
            scall_tok[i] = cs.n
        comb_tok = {}
        for i in range(NTT):
            SCORES = SCALL[:, i * 16:(i + 1) * 16]
            _w(V, (cs, scall_tok[i]), (cv, cv.n))
            cv.inc(V.tensor_add(SC2[:], SCORES[:], BE[:]))
            for g in range(4):
                _w(V, (cv, cv.n))
                sl = SC2[:, 4 * g:4 * g + 4]
                cv.inc(V.reduce_max(RSC[:, 0:1], sl, axis=AX.X))
                _w(V, (cv, cv.n))
                cv.inc(V.tensor_scalar(SMASK[:, 0:4], sl, RSC[:, 0:1], None,
                                       ALU.is_equal))
                _w(V, (cv, cv.n))
                cv.inc(V.scalar_tensor_tensor(SMASK[:, 4:8], SMASK[:, 0:4], -BIG, sl,
                                              ALU.mult, ALU.add))
                _w(V, (cv, cv.n))
                cv.inc(V.reduce_max(RSC[:, 1:2], SMASK[:, 4:8], axis=AX.X))
                _w(V, (cv, cv.n))
                cv.inc(V.tensor_add(GS[:, g:g + 1], RSC[:, 0:1], RSC[:, 1:2]))
            _w(V, (cv, cv.n))
            cv.inc(V.reduce_max(RSC[:, 0:1], GS[:], axis=AX.X))
            _w(V, (cv, cv.n))
            cv.inc(V.tensor_scalar(GM[:], GS[:], RSC[:, 0:1], None, ALU.is_equal))
            _w(V, (cv, cv.n))
            cv.inc(V.scalar_tensor_tensor(GM[:], GM[:], -BIG, GS[:], ALU.mult, ALU.add))
            _w(V, (cv, cv.n))
            cv.inc(V.reduce_max(RSC[:, 1:2], GM[:], axis=AX.X))
            _w(V, (cv, cv.n))
            cv.inc(V.tensor_scalar(GM[:], GS[:], RSC[:, 1:2], None, ALU.is_ge))
            _w(V, (cv, cv.n))
            for g in range(4):
                cv.inc(V.tensor_scalar_mul(SMASK[:, 4 * g:4 * g + 4], ON4[:],
                                           GM[:, g:g + 1]))
            _w(V, (cv, cv.n))
            cv.inc(V.tensor_mul(SC2[:], SC2[:], SMASK[:]))
            cv.inc(V.memset(SEL[:], 0.0))
            for it in range(4):
                _w(V, (cv, cv.n))
                cv.inc(V.reduce_max(RSC[:, 0:1], SC2[:], axis=AX.X))
                _w(V, (cv, cv.n))
                cv.inc(V.tensor_scalar(SMASK[:], SC2[:], RSC[:, 0:1], None,
                                       ALU.is_equal))
                _w(V, (cv, cv.n))
                cv.inc(V.tensor_mul(SMASK[:], SMASK[:], WSEL[:]))
                _w(V, (cv, cv.n))
                cv.inc(V.reduce_max(RSC[:, 1:2], SMASK[:], axis=AX.X))
                _w(V, (cv, cv.n))
                cv.inc(V.tensor_scalar(SMASK[:], SMASK[:], RSC[:, 1:2], None,
                                       ALU.is_equal))
                _w(V, (cv, cv.n))
                cv.inc(V.tensor_add(SEL[:], SEL[:], SMASK[:]))
                cv.inc(V.scalar_tensor_tensor(SC2[:], SMASK[:], -BIG, SC2[:],
                                              ALU.mult, ALU.add))
            _w(V, (cv, cv.n))
            cv.inc(V.tensor_mul(SEL[:], SEL[:], SCORES[:]))
            _w(V, (cv, cv.n))
            cv.inc(V.reduce_sum(RSC[:, 0:1], SEL[:], axis=AX.X))
            _w(V, (cv, cv.n))
            cv.inc(V.tensor_scalar_add(RSC[:, 0:1], RSC[:, 0:1], 1e-20))
            _w(V, (cv, cv.n))
            cv.inc(V.reciprocal(RSC[:, 1:2], RSC[:, 0:1]))
            _w(V, (cv, cv.n))
            comb_tok[i] = cv.inc(V.tensor_scalar(COMB[:, i * 16:(i + 1) * 16],
                                                 SEL[:], RSC[:, 1:2],
                                                 2.5, ALU.mult, ALU.mult))
        t_comb = cv.n

        moe_done = {}
        for e in range(20):
            shared = e >= 16
            c = e - 16
            if e >= 2:
                _w(G, (ct, moe_done[e - 2]))
            else:
                _w(G, (ct, ct.n))
            if shared:
                cg.inc(G.dma_start(WU[e % 2][:],
                                   sup_h[:, c * 512:(c + 1) * 512]
                                   .rearrange("(k p) f -> p k f", p=128)), 16)
                cg.inc(G.dma_start(WD[e % 2],
                                   sdn_h[c * 512:(c + 1) * 512, :]
                                   .rearrange("(k p) d -> p k d", p=128)), 16)
            else:
                cg.inc(G.dma_start(WU[e % 2][:],
                                   up_h[e, :, :].rearrange("(k p) f -> p k f", p=128)), 16)
                cg.inc(G.dma_start(WD[e % 2],
                                   dn_h[e, :, :].rearrange("(k p) d -> p k d", p=128)), 16)
            t_wl = cg.n
            _w(T, (cg, t_wl), (cv, t_attn))
            for ft in range(4):
                for tb in range(4):
                    nb = (ft * 4 + tb) % 2
                    pb = P[nb]
                    _w(T, (cs, pfs[nb]), (cv, pfv[nb]))
                    for k in range(ND):
                        inst = T.matmul(pb[:],
                                        WU[e % 2][:, k * 512 + ft * 128:
                                                  k * 512 + (ft + 1) * 128],
                                        XNT[:, k * T_CORE + tb * 512:
                                            k * T_CORE + (tb + 1) * 512],
                                        start=(k == 0), stop=(k == ND - 1))
                    ct.inc(inst)
                    _w(S, (ct, ct.n))
                    if ft < 3:
                        sl = HF[:, ft * T_CORE + tb * 512:ft * T_CORE + (tb + 1) * 512]
                    else:
                        sl = SCR[:, tb * 512:(tb + 1) * 512]
                    cs.inc(S.activation(sl, pb[:], AF.Relu))
                    _w(S, (cs, cs.n))
                    pfs[nb] = cs.inc(
                        S.activation(AT[:, HBOF + ft * T_CORE + tb * 512:
                                        HBOF + ft * T_CORE + (tb + 1) * 512],
                                     sl, AF.Square))
            t_h = cs.n
            _w(T, (cs, t_h))
            for i in range(NTT):
                for dh in range(2):
                    nb = 2 + (2 * i + dh) % 2
                    pb = P[nb]
                    _w(T, (cv, pfv[nb]), (cs, pfs[nb]))
                    for ft in range(4):
                        inst = T.matmul(pb[:],
                                        AT[:, HBOF + ft * T_CORE + i * 128:
                                           HBOF + ft * T_CORE + (i + 1) * 128],
                                        WD[e % 2][:, ft * D + dh * 512:
                                                  ft * D + (dh + 1) * 512],
                                        start=(ft == 0), stop=(ft == 3))
                    ct.inc(inst)
                    _w(V, (ct, ct.n))
                    if not shared:
                        _w(V, (cv, comb_tok[i]))
                    sl = ACC[:, i * D + dh * 512:i * D + (dh + 1) * 512]
                    if shared:
                        pfv[nb] = cv.inc(V.tensor_tensor(sl, sl, pb[:], ALU.add))
                    else:
                        pfv[nb] = cv.inc(V.scalar_tensor_tensor(
                            sl, pb[:], COMB[:, i * 16 + e:i * 16 + e + 1], sl,
                            ALU.mult, ALU.add))
            moe_done[e] = ct.n
        t_moe = cv.n

        _w(G, (cv, t_moe))
        for i in range(4):
            co.inc(G.dma_start(out1_h[i * 512:(i + 1) * 512, :]
                               .rearrange("(n p) d -> p n d", p=128),
                               ACC[:, i * 4 * D:(i + 1) * 4 * D]), 16)
        t_ln1 = layernorm_and_transpose(t_moe, 0)
        _w(G, (cv, t_ln1))
        co.inc(G.dma_start(xn1_h[:, :].rearrange("(k p) t -> p k t", p=128),
                           XNT[:]), 16)

        _w(G, (ct, ct.n))
        cg.inc(G.dma_start(W1V,
                           w1_h[:, 2048:3072].rearrange("(k p) e -> p k e", p=128)), 16)
        _w(Y, (ct, ct.n), (cs, cs.n))
        cy.inc(Y.dma_start(B1V[:], b1v_h[:, :]), 16)
        t_w1v = cg.n
        for which in range(2):
            cofs = 0 if which == 0 else 1024
            out_hbm = qs_h if which == 0 else kt_h
            for e in range(ND):
                _w(G, (ct, ct.n))
                cg.inc(G.dma_start(WQS[e % 2],
                                   w1_h[:, cofs + e * 128:cofs + (e + 1) * 128]
                                   .rearrange("(k p) c -> p k c", p=128)), 16)
                _w(T, (cg, cg.n), (cv, t_ln1), (co, co.n))
                stg = STG[e % 2]
                for tb in range(4):
                    nb = tb % 2
                    pb = P[nb]
                    _w(T, (cs, pfs[nb]), (cv, pfv[nb]))
                    for k in range(ND):
                        inst = T.matmul(pb[:], WQS[e % 2][:, k * 128:(k + 1) * 128],
                                        XNT[:, k * T_CORE + tb * 512:
                                            k * T_CORE + (tb + 1) * 512],
                                        start=(k == 0), stop=(k == ND - 1))
                    ct.inc(inst)
                    _w(S, (ct, ct.n))
                    if which == 0:
                        pfs[nb] = cs.inc(S.activation(stg[:, tb * 512:(tb + 1) * 512],
                                            pb[:], AF.Identity, bias=B1Q32[:, e:e + 1],
                                            scale=1.0 / 32.0))
                    else:
                        pfs[nb] = cs.inc(S.activation(stg[:, tb * 512:(tb + 1) * 512],
                                            pb[:], AF.Identity,
                                            bias=B1QK[:, 8 + e:8 + e + 1]))
                _w(G, (cs, cs.n))
                co.inc(G.dma_start(out_hbm[e * 128:(e + 1) * 128, :], stg[:]), 16)
        _w(T, (cg, t_w1v))
        for i in range(NTT):
            stg = STG[i % 2]
            _w(T, (co, co.n))
            for eh in range(2):
                nb = (2 * i + eh) % 2
                pb = P[nb]
                _w(T, (cv, pfv[nb]), (cs, pfs[nb]))
                for k in range(ND):
                    T.matmul(pb[:], XNT[:, k * T_CORE + i * 128:k * T_CORE + (i + 1) * 128],
                             W1V[:, k * D + eh * 512:k * D + (eh + 1) * 512],
                             start=(k == 0), stop=False)
                ct.inc(T.matmul(pb[:], ONR[:, :], B1V[0:1, eh * 512:(eh + 1) * 512],
                                start=False, stop=True))
                _w(V, (ct, ct.n))
                pfv[nb] = cv.inc(V.tensor_copy(stg[:, eh * 512:(eh + 1) * 512], pb[:]))
            _w(G, (cv, cv.n))
            co.inc(G.dma_start(vo_h[i * 128:(i + 1) * 128, :], stg[:, 0:1024]), 16)
        G.wait_ge(co.h, co.n)
    return nc



def build_B(j, attn_only=False):
    nc = bass.Bass("TRN2", target_bir_lowering=False)
    dt = nc.dram_tensor
    if attn_only:
        dbg_negc = dt("dbg_negc", [1, 512], F32, kind="ExternalOutput")
        dbg_lr = dt("dbg_lr", [1, 4], F32, kind="ExternalOutput")
        dbg_yp = dt("dbg_yp", [128, 4096], F32, kind="ExternalOutput")
        dbg_pts = dt("dbg_pts", [128, 4096], BF16, kind="ExternalOutput")
    out1_h = dt("out1", [T_CORE, D], F32, kind="ExternalInput")
    xn1_h = dt("xn1T", [D, T_CORE], BF16, kind="ExternalInput")
    qs_h = dt("qsT", [D, T_CORE], BF16, kind="ExternalInput")
    ktf_h = dt("ktF", [D, 8192], BF16, kind="ExternalInput")
    vf_h = dt("vF", [8192, D], BF16, kind="ExternalInput")
    g1_h = dt("g1T", [D, 15], F32, kind="ExternalInput")
    f1w_h = dt("f1T", [15, D, 2048], F32, kind="ExternalInput")
    f1b_h = dt("f1b", [15, 128, 16], F32, kind="ExternalInput")
    geg_h = dt("geg", [15, 128, 8], F32, kind="ExternalInput")
    f2w_h = dt("f2T", [15, D, D], F32, kind="ExternalInput")
    fba_h = dt("fbaug", [16, D], F32, kind="ExternalInput")
    s1w_h = dt("s1P", [16, D, 256], F32, kind="ExternalInput")
    s1b_h = dt("s1b", [128, 32], F32, kind="ExternalInput")
    sgg_h = dt("sgeg", [128, 16], F32, kind="ExternalInput")
    s2w_h = dt("s2T", [2048, D], F32, kind="ExternalInput")
    msk_h = dt("masks", [128, 2048], F32, kind="ExternalInput")
    idb_h = dt("idb", [128, 128], F32, kind="ExternalInput")
    onesr_h = dt("onesr", [1, 128], F32, kind="ExternalInput")
    ones4_h = dt("ones4", [128, 4], F32, kind="ExternalInput")
    out_h = dt("out", [T_CORE, D], F32, kind="ExternalOutput")

    cA, cB = j, 7 - j
    qi_starts = [cA * 1024, cA * 1024 + 512, cB * 1024, cB * 1024 + 512]
    tile_base = [0, 4, 8, 12]
    nchunks_max = 8 - j

    st = ExitStack()
    with st:
        sb = lambda n, s, d_: st.enter_context(nc.sbuf_tensor(n, s, d_))
        pst = lambda n, s, d_: st.enter_context(nc.psum_tensor(n, s, d_))
        ACC = sb("ACC", [128, NTT * D], F32)
        XNT = sb("XNT", [128, ND * T_CORE], BF16)
        AR1 = sb("AR1", [128, 16384], BF16)
        AR2 = sb("AR2", [128, 16384], BF16)
        AR3 = sb("AR3", [128, 8192], BF16)
        AR4 = sb("AR4", [128, 7168], F32)
        IDF = sb("IDF", [128, 128], F32)
        ONR = sb("ONR", [1, 128], F32)
        ON4 = sb("ON4", [128, 4], F32)
        ONB = sb("ONB", [1, 128], BF16)
        ONCB = sb("ONCB", [128, 1], BF16)
        NEGCB = sb("NEGCB", [1, 512], BF16)
        LC = sb("LC", [128, 1], F32)
        RI = sb("RI", [128, 4], F32)
        RSC = sb("RSC", [128, 8], F32)
        ESC = sb("ESC", [128, 15], F32)
        SELB = sb("SELB", [128, 15], F32)
        TMP15 = sb("TMP15", [128, 15], F32)
        COMB1 = sb("COMB1", [128, NTT * 16], F32)
        F1B = AR4[:, 5120:5136]
        GEG = AR4[:, 5136:5144]
        S1B = AR4[:, 5152:5184]
        SGG = AR4[:, 5184:5200]
        G1W = sb("G1W", [128, 8 * 15], BF16)
        P = [pst(f"P{i}", [128, 512], F32) for i in range(4)]
        PLL = pst("PLL", [1, 512], F32)
        PLC = pst("PLC", [128, 4], F32)
        PCT = pst("PCT", [16, 128], F32)

        KC = [AR1[:, 0:8192], AR1[:, 8192:16384]]
        VC = [AR2[:, 0:8192], AR2[:, 8192:16384]]
        QSG = AR3[:, 0:4096]
        PTS = AR3[:, 4096:8192]
        YP = AR4[:, 0:4096]
        MSK = AR4[:, 4096:6144]
        TMS = AR4[:, 6144:6656]
        SQ = AR4[:, 6144:6656]
        LR = AR4[0:1, 6656:7168]
        NQR = AR4[32:33, 6656:7168]
        NEGC = AR4[64:65, 6656:7168]
        MKR = AR4[0:1, 6144:6656]

        cg = Ch(nc, st, "cg")
        cy = Ch(nc, st, "cy")
        ct = Ch(nc, st, "ct")
        cv = Ch(nc, st, "cv")
        cs = Ch(nc, st, "cs")
        co = Ch(nc, st, "co")
        G, Y, T, V, S = nc.gpsimd, nc.sync, nc.tensor, nc.vector, nc.scalar

        for i in range(4):
            cy.inc(Y.dma_start(ACC[:, i * 4 * D:(i + 1) * 4 * D],
                               out1_h[i * 512:(i + 1) * 512, :]
                               .rearrange("(n p) d -> p n d", p=128)), 16)
        cy.inc(Y.dma_start(XNT[:], xn1_h[:, :].rearrange("(k p) t -> p k t", p=128)), 16)
        cy.inc(Y.dma_start(MSK[:], msk_h[:, :]), 16)
        cy.inc(Y.dma_start(IDF[:], idb_h[:, :]), 16)
        cy.inc(Y.dma_start(ONR[:], onesr_h[:, :]), 16)
        cy.inc(Y.dma_start(ON4[:], ones4_h[:, :]), 16)
        t_loads = cy.n
        _w(V, (cy, t_loads))
        cv.inc(V.tensor_copy(ONB[:], ONR[:, :]))
        cv.inc(V.tensor_copy(ONCB[:], ON4[:, 0:1]))
        t_ones = cv.n

        for c in range(nchunks_max):
            _w(Y, (ct, ct.n))
            cy.inc(Y.dma_start(KC[c % 2][:],
                               ktf_h[:, c * 1024:(c + 1) * 1024]
                               .rearrange("(k p) t -> p k t", p=128)), 16)
            t_kc = cy.n
            for half in range(2):
                pbap = PLL[:] if half == 0 else P[2][0:1, 0:512]
                for k in range(ND):
                    _w(V, (cy, t_kc), (ct, ct.n), (cv, cv.n))
                    sl = KC[c % 2][:, k * 1024 + half * 512:k * 1024 + (half + 1) * 512]
                    cv.inc(V.tensor_mul(SQ[:], sl, sl))
                    _w(T, (cv, cv.n))
                    ct.inc(T.matmul(pbap, ON4[:, 0:1], SQ[:],
                                    start=(k == 0), stop=(k == ND - 1),
                                    skip_group_check=True))
            _w(V, (ct, ct.n))
            cv.inc(V.tensor_copy(MKR[0:1, 0:512], PLL[:]))
            _w(V, (cv, cv.n))
            cv.inc(V.tensor_tensor(MKR[0:1, 0:512], MKR[0:1, 0:512],
                                   P[2][0:1, 0:512], ALU.max))
            _w(V, (cv, cv.n))
            cv.inc(V.reduce_max(RSC[0:1, 0:1], MKR[0:1, 0:512], axis=AX.X))
            _w(V, (cv, cv.n))
            if c == 0:
                cv.inc(V.tensor_copy(RSC[0:1, 1:2], RSC[0:1, 0:1]))
            else:
                cv.inc(V.tensor_tensor(RSC[0:1, 1:2], RSC[0:1, 1:2], RSC[0:1, 0:1],
                                       ALU.max))
        _w(S, (cv, cv.n))
        cs.inc(S.activation(RSC[0:1, 2:3], RSC[0:1, 1:2], AF.Sqrt))
        t_mk = cs.n

        yfree = {}
        stfree = {}
        for g in range(4):
            qi0 = qi_starts[g]
            nkb = qi0 // 128 + 4
            nch = (nkb + 7) // 8
            _w(Y, (ct, ct.n))
            cy.inc(Y.dma_start(QSG[:],
                               qs_h[:, g * 512:(g + 1) * 512]
                               .rearrange("(k p) t -> p k t", p=128)), 16)
            t_qs = cy.n
            for k in range(ND):
                _w(V, (cy, t_qs), (ct, ct.n), (cv, cv.n))
                sl = QSG[:, k * 512:(k + 1) * 512]
                cv.inc(V.tensor_mul(SQ[:], sl, sl))
                _w(T, (cv, cv.n))
                ct.inc(T.matmul(PLL[:], ON4[:, 0:1], SQ[:],
                                start=(k == 0), stop=(k == ND - 1),
                                skip_group_check=True))
            _w(V, (ct, ct.n))
            cv.inc(V.tensor_copy(NQR[0:1, 0:512], PLL[:]))
            _w(S, (cv, cv.n), (cs, t_mk))
            cs.inc(S.activation(NQR[0:1, 0:512], NQR[0:1, 0:512], AF.Sqrt))
            _w(V, (cs, cs.n))
            cv.inc(V.tensor_scalar(NEGC[0:1, 0:512], NQR[0:1, 0:512],
                                   RSC[0:1, 2:3], -1.01, ALU.mult, ALU.mult))
            _w(V, (cv, cv.n))
            t_negc = cv.inc(V.tensor_copy(NEGCB[:], NEGC[0:1, 0:512]))
            y_evac_tok = cv.n
            pt_done = {}
            ymm_done = {}
            t_g0 = ct.n
            for c in range(nch):
                _w(Y, (ct, ymm_done.get(c - 2, t_g0)))
                cy.inc(Y.dma_start(KC[c % 2][:],
                                   ktf_h[:, c * 1024:(c + 1) * 1024]
                                   .rearrange("(k p) t -> p k t", p=128)), 16)
                cy.inc(Y.dma_start(VC[c % 2][:],
                                   vf_h[c * 1024:(c + 1) * 1024, :]
                                   .rearrange("(k p) d -> p k d", p=128)), 16)
                t_cld = cy.n
                kbs = [kb for kb in range(8) if 8 * c + kb < nkb]
                for kb in kbs:
                    kbg = 8 * c + kb
                    pb = P[kb % 2]
                    _w(T, (cs, stfree.get(kb % 2, 0)), (cy, t_cld), (cv, t_negc))
                    for k in range(ND):
                        T.matmul(pb[:], KC[c % 2][:, k * 1024 + kb * 128:
                                                  k * 1024 + (kb + 1) * 128],
                                 QSG[:, k * 512:(k + 1) * 512],
                                 start=(k == 0), stop=False)
                    ct.inc(T.matmul(pb[:], ONB[:, :], NEGCB[0:1, :],
                                    start=False, stop=True))
                    st_tok = ct.n
                    dst = PTS[:, kb * 512:(kb + 1) * 512]
                    if kbg * 128 >= qi0:
                        m = (kbg * 128 - qi0) // 128
                        _w(V, (ct, st_tok), (cs, cs.n))
                        cv.inc(V.tensor_tensor(TMS[:], pb[:], MSK[:, m * 512:
                                                                  (m + 1) * 512],
                                               ALU.add))
                        _w(S, (cv, cv.n), (ct, ymm_done.get(c - 1, 0)))
                        cs.inc(S.activation(dst, TMS[:], AF.Exp))
                    else:
                        _w(S, (ct, st_tok), (ct, ymm_done.get(c - 1, 0)))
                        cs.inc(S.activation(dst, pb[:], AF.Exp))
                    stfree[kb % 2] = cs.n
                    pt_done[kbg] = cs.n
                    _w(T, (cs, pt_done[kbg]))
                    for sub in range(4):
                        inst = T.matmul(PLC[:, sub:sub + 1],
                                        PTS[:, kb * 512 + sub * 128:
                                            kb * 512 + (sub + 1) * 128],
                                        ONCB[:],
                                        start=(kbg == 0 and sub == 0),
                                        stop=(kbg == nkb - 1 and sub == 3),
                                        skip_group_check=True)
                    ct.inc(inst)
                for sub in range(4):
                    for dh in range(2):
                        nb = 2 + (sub * 2 + dh) % 2
                        pb = P[nb]
                        _w(T, (cv, yfree.get(nb, 0)))
                        for n_i, kb in enumerate(kbs):
                            inst = T.matmul(pb[:], PTS[:, kb * 512 + sub * 128:
                                              kb * 512 + (sub + 1) * 128],
                                     VC[c % 2][:, kb * 1024 + dh * 512:
                                               kb * 1024 + (dh + 1) * 512],
                                     start=(n_i == 0), stop=(n_i == len(kbs) - 1))
                        ct.inc(inst)
                        _w(V, (ct, ct.n))
                        dsty = YP[:, sub * 1024 + dh * 512:sub * 1024 + (dh + 1) * 512]
                        if c == 0:
                            yfree[nb] = cv.inc(V.tensor_copy(dsty, pb[:]))
                        else:
                            yfree[nb] = cv.inc(V.tensor_tensor(dsty, dsty, pb[:],
                                                               ALU.add))
                        y_evac_tok = cv.n
                ymm_done[c] = ct.n
            _w(V, (ct, ct.n))
            for sub in range(4):
                cv.inc(V.reciprocal(RI[:, sub:sub + 1], PLC[:, sub:sub + 1]))
            for sub in range(4):
                _w(V, (cv, cv.n))
                tt = tile_base[g] + sub
                sl = ACC[:, tt * D:(tt + 1) * D]
                cv.inc(V.scalar_tensor_tensor(sl, YP[:, sub * 1024:(sub + 1) * 1024],
                                              RI[:, sub:sub + 1], sl,
                                              ALU.mult, ALU.add))
        t_attn_done_v = cv.n
        t_attn_done_t = ct.n
        t_attn_done_s = cs.n
        if attn_only:
            _w(G, (cv, cv.n))
            co.inc(G.dma_start(dbg_negc[:, :], NEGC), 16)
            co.inc(G.dma_start(dbg_lr[:, :], RI[0:1, 0:4]), 16)
            co.inc(G.dma_start(dbg_yp[:, :], YP), 16)
            co.inc(G.dma_start(dbg_pts[:, :], PTS), 16)
            for i in range(4):
                co.inc(G.dma_start(out_h[i * 512:(i + 1) * 512, :]
                                   .rearrange("(n p) d -> p n d", p=128),
                                   ACC[:, i * 4 * D:(i + 1) * 4 * D]), 16)
            G.wait_ge(co.h, co.n)
            return nc

        cg.inc(G.dma_start(G1W[:], g1_h[:, :].rearrange("(k p) e -> p k e", p=128)), 16)
        t_g1 = cg.n
        for i in range(NTT):
            _w(T, (cg, t_g1), (cs, cs.n), (cv, cv.n))
            for k in range(ND):
                inst = T.matmul(P[0][:, 0:15],
                                XNT[:, k * T_CORE + i * 128:k * T_CORE + (i + 1) * 128],
                                G1W[:, k * 15:(k + 1) * 15],
                                start=(k == 0), stop=(k == ND - 1))
            ct.inc(inst)
            _w(V, (ct, ct.n))
            cv.inc(V.reduce_max(RSC[:, 0:1], P[0][:, 0:15], axis=AX.X, negate=True))
            _w(S, (cv, cv.n))
            cs.inc(S.activation(ESC[:], P[0][:, 0:15], AF.Exp, bias=RSC[:, 0:1]))
            _w(V, (cs, cs.n))
            cv.inc(V.tensor_copy(SELB[:], ESC[:]))
            for it in range(3):
                _w(V, (cv, cv.n))
                cv.inc(V.reduce_max(RSC[:, 1:2], SELB[:], axis=AX.X))
                if it < 2:
                    _w(V, (cv, cv.n))
                    cv.inc(V.tensor_scalar(TMP15[:], SELB[:], RSC[:, 1:2], None,
                                           ALU.is_equal))
                    _w(V, (cv, cv.n))
                    cv.inc(V.scalar_tensor_tensor(SELB[:], TMP15[:], -BIG, SELB[:],
                                                  ALU.mult, ALU.add))
            _w(V, (cv, cv.n))
            cv.inc(V.tensor_scalar(TMP15[:], ESC[:], RSC[:, 1:2], None, ALU.is_ge))
            _w(V, (cv, cv.n))
            cv.inc(V.tensor_mul(TMP15[:], TMP15[:], ESC[:]))
            _w(V, (cv, cv.n))
            cv.inc(V.reduce_sum(RSC[:, 2:3], TMP15[:], axis=AX.X))
            _w(V, (cv, cv.n))
            cv.inc(V.tensor_scalar_add(RSC[:, 2:3], RSC[:, 2:3], 1e-20))
            _w(V, (cv, cv.n))
            cv.inc(V.reciprocal(RSC[:, 3:4], RSC[:, 2:3]))
            _w(V, (cv, cv.n))
            cv.inc(V.tensor_scalar_mul(COMB1[:, i * 16:i * 16 + 15], TMP15[:],
                                       RSC[:, 3:4]))
            cv.inc(V.memset(COMB1[:, i * 16 + 15:i * 16 + 16], 1.0))
            _w(T, (cv, cv.n))
            ct.inc(T.transpose(PCT[:], COMB1[:, i * 16:(i + 1) * 16], IDF[:]))
            _w(V, (ct, ct.n))
            cv.inc(V.tensor_copy(AR4[0:16, 2048 + i * 128:2048 + (i + 1) * 128],
                                 PCT[:]))
        t_comb1 = cv.n
        CT = AR4[0:16, 2048:4096]
        FB = AR4[0:16, 4096:5120]
        HP = AR4[:, 0:1024]
        GL = AR4[:, 1024:1536]
        FW = AR1[:, 0:16384]
        FC2 = AR2[:, 0:8192]
        AB = AR2[:, 8192:12288]
        SF2 = AR1[:, 0:16384]
        SAB = AR2[:, 0:8192]
        SW = [AR2[:, 8192 + i * 2048:8192 + (i + 1) * 2048] for i in range(2)]

        _w(Y, (cv, t_attn_done_v), (cs, t_attn_done_s))
        cy.inc(Y.dma_start(AR4[0:16, 4096:5120], fba_h[:, :]), 16)
        t_fb = cy.n
        for i in range(NTT):
            for dh in range(2):
                pb = P[1 + (2 * i + dh) % 2]
                _w(T, (cv, t_comb1), (cy, t_fb), (cv, cv.n))
                ct.inc(T.matmul(pb[:], CT[:, i * 128:(i + 1) * 128],
                                FB[:, dh * 512:(dh + 1) * 512],
                                start=True, stop=True))
                _w(V, (ct, ct.n))
                sl = ACC[:, i * D + dh * 512:i * D + (dh + 1) * 512]
                cv.inc(V.tensor_tensor(sl, sl, pb[:], ALU.add))

        ffs = {nb: cs.n for nb in range(4)}
        ffv = {nb: cv.n for nb in range(4)}
        fc2_done = {}
        for e in range(15):
            _w(G, (ct, ct.n))
            cg.inc(G.dma_start(FW, f1w_h[e, :, :]
                               .rearrange("(k p) f -> p k f", p=128)), 16)
            cg.inc(G.dma_start(FC2, f2w_h[e, :, :]
                               .rearrange("(k p) d -> p k d", p=128)), 16)
            _w(Y, (cs, cs.n))
            cy.inc(Y.dma_start(F1B[:], f1b_h[e, :, :]), 16)
            cy.inc(Y.dma_start(GEG[:], geg_h[e, :, :]), 16)
            t_wl, t_bl = cg.n, cy.n
            for tc in range(4):
                t_abw = fc2_done.get(tc - 1, 0)
                for fp in range(8):
                    for half in range(2):
                        nb = (fp * 2 + half) % 2
                        pb = P[nb]
                        fti = fp + 8 * half
                        _w(T, (cg, t_wl), (cs, ffs.get(nb, 0)))
                        for k in range(ND):
                            inst = T.matmul(pb[:],
                                            FW[:, k * 2048 + fti * 128:
                                               k * 2048 + (fti + 1) * 128],
                                            XNT[:, k * T_CORE + tc * 512:
                                                k * T_CORE + (tc + 1) * 512],
                                            start=(k == 0), stop=(k == ND - 1))
                        ct.inc(inst)
                        _w(S, (ct, ct.n), (cy, t_bl))
                        ffs[nb] = cs.inc(S.activation(
                            HP[:, half * 512:(half + 1) * 512], pb[:],
                            AF.Identity, bias=F1B[:, fti:fti + 1]))
                    _w(S, (cs, cs.n))
                    cs.inc(S.activation(GL[:], HP[:, 512:1024], AF.Gelu))
                    _w(V, (cs, cs.n), (ct, t_abw))
                    cv.inc(V.scalar_tensor_tensor(AB[:, fp * 512:(fp + 1) * 512],
                                                  GL[:], GEG[:, fp:fp + 1],
                                                  HP[:, 0:512], ALU.mult, ALU.mult))
                t_a = cv.n
                _w(T, (cv, t_a))
                for ts4 in range(4):
                    for dh in range(2):
                        nb = 2 + (ts4 * 2 + dh) % 2
                        pb = P[nb]
                        _w(T, (cv, ffv.get(nb, 0)))
                        for ft in range(8):
                            inst = T.matmul(pb[:],
                                            AB[:, ft * 512 + ts4 * 128:
                                               ft * 512 + (ts4 + 1) * 128],
                                            FC2[:, ft * D + dh * 512:
                                                ft * D + (dh + 1) * 512],
                                            start=(ft == 0), stop=(ft == 7))
                        ct.inc(inst)
                        _w(V, (ct, ct.n))
                        i = tc * 4 + ts4
                        sl = ACC[:, i * D + dh * 512:i * D + (dh + 1) * 512]
                        ffv[nb] = cv.inc(V.scalar_tensor_tensor(
                            sl, pb[:], COMB1[:, i * 16 + e:i * 16 + e + 1], sl,
                            ALU.mult, ALU.add))
                fc2_done[tc] = ct.n

        _w(G, (ct, ct.n))
        cg.inc(G.dma_start(SF2, s2w_h[:, :].rearrange("(k p) d -> p k d", p=128)), 16)
        cy.inc(Y.dma_start(S1B[:], s1b_h[:, :]), 16)
        cy.inc(Y.dma_start(SGG[:], sgg_h[:, :]), 16)
        t_sw, t_sb = cg.n, cy.n
        sfc2_done = {}
        for tc in range(4):
            t_abw = sfc2_done.get(tc - 1, 0)
            for i in range(16):
                _w(G, (ct, ct.n))
                cg.inc(G.dma_start(SW[i % 2], s1w_h[i, :, :]
                                   .rearrange("(k p) f -> p k f", p=128)), 16)
                t_wl = cg.n
                for half in range(2):
                    nb = (i * 2 + half) % 2
                    pb = P[nb]
                    _w(T, (cg, t_wl), (cs, ffs.get(nb, 0)))
                    for k in range(ND):
                        inst = T.matmul(pb[:],
                                        SW[i % 2][:, k * 256 + half * 128:
                                                  k * 256 + (half + 1) * 128],
                                        XNT[:, k * T_CORE + tc * 512:
                                            k * T_CORE + (tc + 1) * 512],
                                        start=(k == 0), stop=(k == ND - 1))
                    ct.inc(inst)
                    _w(S, (ct, ct.n), (cy, t_sb))
                    ffs[nb] = cs.inc(S.activation(
                        HP[:, half * 512:(half + 1) * 512], pb[:],
                        AF.Identity, bias=S1B[:, 16 * half + i:16 * half + i + 1]))
                _w(S, (cs, cs.n))
                cs.inc(S.activation(GL[:], HP[:, 512:1024], AF.Gelu))
                _w(V, (cs, cs.n), (ct, t_abw))
                cv.inc(V.scalar_tensor_tensor(SAB[:, i * 512:(i + 1) * 512],
                                              GL[:], SGG[:, i:i + 1],
                                              HP[:, 0:512], ALU.mult, ALU.mult))
            t_a = cv.n
            _w(T, (cv, t_a), (cg, t_sw))
            for ts4 in range(4):
                for dh in range(2):
                    nb = 2 + (ts4 * 2 + dh) % 2
                    pb = P[nb]
                    _w(T, (cv, ffv.get(nb, 0)))
                    for ft in range(16):
                        inst = T.matmul(pb[:],
                                        SAB[:, ft * 512 + ts4 * 128:
                                           ft * 512 + (ts4 + 1) * 128],
                                        SF2[:, ft * D + dh * 512:
                                            ft * D + (dh + 1) * 512],
                                        start=(ft == 0), stop=(ft == 15))
                    ct.inc(inst)
                    _w(V, (ct, ct.n))
                    i = tc * 4 + ts4
                    sl = ACC[:, i * D + dh * 512:i * D + (dh + 1) * 512]
                    ffv[nb] = cv.inc(V.tensor_tensor(sl, sl, pb[:], ALU.add))
            sfc2_done[tc] = ct.n
        t_final = cv.n

        _w(G, (cv, t_final))
        for i in range(4):
            co.inc(G.dma_start(out_h[i * 512:(i + 1) * 512, :]
                               .rearrange("(n p) d -> p n d", p=128),
                               ACC[:, i * 4 * D:(i + 1) * 4 * D]), 16)
        G.wait_ge(co.h, co.n)
    return nc



def _np(a):
    return np.ascontiguousarray(np.asarray(a), dtype=np.float32)


def kernel(**kw):
    x = _np(kw["L0_attn_query"])
    W_qkv0 = _np(kw["L0_attn_W_qkv"])
    b_qkv0 = _np(kw["L0_attn_b_qkv"])
    W_out0 = _np(kw["L0_attn_W_out"])
    b_out0 = _np(kw["L0_attn_b_out"])
    wq_w = _np(kw["L0_attn_weight_q"])
    wv_w = _np(kw["L0_attn_weight_v"])
    gate0 = _np(kw["L0_ffn_gate_w"])
    bias_e = _np(kw["L0_ffn_bias_e"])
    up_w = _np(kw["L0_ffn_up_w"])
    down_w = _np(kw["L0_ffn_down_w"])
    sh_up = _np(kw["L0_ffn_shared_up"])
    sh_dn = _np(kw["L0_ffn_shared_down"])
    W_qkv1 = _np(kw["L1_attn_W_qkv"])
    b_qkv1 = _np(kw["L1_attn_b_qkv"])
    gate1 = _np(kw["L1_ffn_gate_w"])
    fc1_w = _np(kw["L1_ffn_fc1_w"])
    fc1_b = _np(kw["L1_ffn_fc1_b"])
    geglu = _np(kw["L1_ffn_geglu_mult"])
    fc2_w = _np(kw["L1_ffn_fc2_w"])
    fc2_b = _np(kw["L1_ffn_fc2_b"])
    s_fc1_w = _np(kw["L1_ffn_s_fc1_w"])
    s_fc1_b = _np(kw["L1_ffn_s_fc1_b"])
    s_geglu = _np(kw["L1_ffn_s_geglu_mult"])
    s_fc2_w = _np(kw["L1_ffn_s_fc2_w"])
    s_fc2_b = _np(kw["L1_ffn_s_fc2_b"])

    C = np.ascontiguousarray

    def pcol(v):
        k = v.size // 128
        return C(v.reshape(k, 128).T)

    shared_A = {
        "wq0T": C(W_qkv0[:1024].T), "wv0T": C(W_qkv0[2048:].T),
        "bq0": pcol(b_qkv0[:1024]), "bv0": pcol(b_qkv0[2048:]),
        "wavq": C(wq_w.transpose(0, 2, 1).reshape(1024, 3)
                  .reshape(8, 128, 3).transpose(1, 0, 2).reshape(128, 24)),
        "wavv": C(wv_w.transpose(0, 2, 1).reshape(1024, 3)
                  .reshape(8, 128, 3).transpose(1, 0, 2).reshape(128, 24)),
        "woT": C(W_out0.T), "bo": C(b_out0[None, :]),
        "gateT": C(gate0.T), "biase": C(np.tile(bias_e[None, :], (128, 1))),
        "upT": C(up_w.transpose(0, 2, 1)), "dnT": C(down_w.transpose(0, 2, 1)),
        "supT": C(sh_up.T), "sdnT": C(sh_dn.T),
        "w1T": C(W_qkv1.T),
        "b1qk": C(np.concatenate([pcol(b_qkv1[:1024]), pcol(b_qkv1[1024:2048])], 1)),
        "b1v": C(b_qkv1[2048:][None, :]),
        "idb": np.eye(128, dtype=np.float32),
        "onesr": np.ones((1, 128), np.float32),
        "ones4": np.ones((128, 4), np.float32),
        "wsel": C(np.tile((2.0 ** (15 - np.arange(16, dtype=np.float32)))[None, :],
                          (128, 1))),
    }
    ii = np.arange(128)[:, None]
    jj = np.arange(512)[None, :]
    masks = np.concatenate(
        [np.where(ii <= jj - 128 * m, 0.0, -BIG).astype(np.float32) for m in range(4)],
        axis=1)
    sf1T = C(s_fc1_w.T)
    s1P = C(np.concatenate([sf1T[:, :2048].reshape(1024, 16, 128),
                            sf1T[:, 2048:].reshape(1024, 16, 128)], axis=2)
            .transpose(1, 0, 2))
    shared_B = {
        "g1T": C(gate1.T),
        "f1T": C(fc1_w.transpose(0, 2, 1)),
        "f1b": C(fc1_b.reshape(15, 16, 128).transpose(0, 2, 1)),
        "geg": C(geglu.reshape(15, 8, 128).transpose(0, 2, 1)),
        "f2T": C(fc2_w.transpose(0, 2, 1)),
        "fbaug": C(np.concatenate([fc2_b, s_fc2_b[None, :]], 0)),
        "s1P": s1P,
        "s1b": C(s_fc1_b.reshape(32, 128).T),
        "sgeg": C(s_geglu.reshape(16, 128).T),
        "s2T": C(s_fc2_w.T),
        "masks": masks,
        "idb": np.eye(128, dtype=np.float32),
        "onesr": np.ones((1, 128), np.float32),
        "ones4": np.ones((128, 4), np.float32),
    }

    ncA = build_A()
    in_maps = []
    for core in range(8):
        b, j = core // 4, core % 4
        xp = np.concatenate([x[b, j * 1024:(j + 1) * 1024],
                             x[b, (7 - j) * 1024:(8 - j) * 1024]], axis=0)
        m = dict(shared_A)
        m["x"] = C(xp)
        in_maps.append(m)
    resA = run_bass_kernel_spmd(ncA, in_maps, core_ids=list(range(8)))
    RA = resA.results

    import ml_dtypes
    ktF = np.zeros((2, D, 8192), dtype=ml_dtypes.bfloat16)
    vF = np.zeros((2, 8192, D), dtype=ml_dtypes.bfloat16)
    for core in range(8):
        b, j = core // 4, core % 4
        kt = RA[core]["kT"]
        vv = RA[core]["vN"]
        ktF[b][:, j * 1024:(j + 1) * 1024] = kt[:, :1024]
        ktF[b][:, (7 - j) * 1024:(8 - j) * 1024] = kt[:, 1024:]
        vF[b][j * 1024:(j + 1) * 1024] = vv[:1024]
        vF[b][(7 - j) * 1024:(8 - j) * 1024] = vv[1024:]

    ncBs = {j: build_B(j) for j in range(4)}
    out = np.empty((2, 8192, D), dtype=np.float32)
    for j in range(4):
        maps = []
        for b in range(2):
            core = b * 4 + j
            m = dict(shared_B)
            m["out1"] = RA[core]["out1"]
            m["xn1T"] = RA[core]["xn1T"]
            m["qsT"] = RA[core]["qsT"]
            m["ktF"] = np.ascontiguousarray(ktF[b])
            m["vF"] = np.ascontiguousarray(vF[b])
            maps.append(m)
        resB = run_bass_kernel_spmd(ncBs[j], maps, core_ids=[j, 4 + j])
        for b in range(2):
            o = resB.results[b]["out"]
            out[b, j * 1024:(j + 1) * 1024] = o[:1024]
            out[b, (7 - j) * 1024:(8 - j) * 1024] = o[1024:]
    return out


# version 06
# speedup vs baseline: 2.8991x; 2.8991x over previous
import math
import numpy as np
from contextlib import ExitStack

import concourse.bass as bass
import concourse.mybir as mybir
from concourse.bass_utils import run_bass_kernel_spmd

F32 = mybir.dt.float32
BF16 = mybir.dt.bfloat16
AF = mybir.ActivationFunctionType
ALU = mybir.AluOpType
AX = mybir.AxisListType

T_CORE = 2048
NTT = 16
ND = 8
D = 1024
BIG = 1.0e30
ISQ2 = 0.7071067811865476


class Ch:
    def __init__(self, nc, st, name):
        self.h = st.enter_context(nc.semaphore(name))
        self.n = 0

    def inc(self, instr, k=1):
        instr.then_inc(self.h, k)
        self.n += k
        return self.n


def _w(eng, *tok):
    for ch, v in tok:
        if v > 0:
            eng.wait_ge(ch.h, v)



def build_A():
    nc = bass.Bass("TRN2", target_bir_lowering=False)
    dt = nc.dram_tensor
    x_h = dt("x", [T_CORE, D], F32, kind="ExternalInput")
    wq0_h = dt("wq0T", [D, D], F32, kind="ExternalInput")
    wv0_h = dt("wv0T", [D, D], F32, kind="ExternalInput")
    bq0_h = dt("bq0", [128, 8], F32, kind="ExternalInput")
    bv0_h = dt("bv0", [128, 8], F32, kind="ExternalInput")
    wvq_h = dt("wavq", [128, 24], F32, kind="ExternalInput")
    wvv_h = dt("wavv", [128, 24], F32, kind="ExternalInput")
    wo_h = dt("woT", [D, D], F32, kind="ExternalInput")
    bo_h = dt("bo", [1, D], F32, kind="ExternalInput")
    gw_h = dt("gateT", [D, 16], F32, kind="ExternalInput")
    be_h = dt("biase", [128, 16], F32, kind="ExternalInput")
    up_h = dt("upT", [16, D, 512], F32, kind="ExternalInput")
    dn_h = dt("dnT", [16, 512, D], F32, kind="ExternalInput")
    sup_h = dt("supT", [D, 2048], F32, kind="ExternalInput")
    sdn_h = dt("sdnT", [2048, D], F32, kind="ExternalInput")
    w1_h = dt("w1T", [D, 3072], F32, kind="ExternalInput")
    b1qk_h = dt("b1qk", [128, 16], F32, kind="ExternalInput")
    b1v_h = dt("b1v", [1, D], F32, kind="ExternalInput")
    idb_h = dt("idb", [128, 128], F32, kind="ExternalInput")
    onesr_h = dt("onesr", [1, 128], F32, kind="ExternalInput")
    ones4_h = dt("ones4", [128, 4], F32, kind="ExternalInput")
    wsel_h = dt("wsel", [128, 16], F32, kind="ExternalInput")
    out1_h = dt("out1", [T_CORE, D], F32, kind="ExternalOutput")
    xn1_h = dt("xn1T", [D, T_CORE], BF16, kind="ExternalOutput")
    qs_h = dt("qsT", [D, T_CORE], BF16, kind="ExternalOutput")
    kt_h = dt("kT", [D, T_CORE], BF16, kind="ExternalOutput")
    vo_h = dt("vN", [T_CORE, D], BF16, kind="ExternalOutput")

    st = ExitStack()
    with st:
        sb = lambda n, s, d_: st.enter_context(nc.sbuf_tensor(n, s, d_))
        pst = lambda n, s, d_: st.enter_context(nc.psum_tensor(n, s, d_))
        ACC = sb("ACC", [128, NTT * D], F32)
        XNT = sb("XNT", [128, ND * T_CORE], BF16)
        AT = sb("AT", [128, ND * T_CORE], BF16)
        HF = sb("HF", [128, 6144], F32)
        SCR = sb("SCR", [128, 4096], F32)
        WU = [sb(f"WU{i}", [128, 8 * 512], BF16) for i in range(2)]
        WBIG = sb("WBIG", [128, 8 * D], BF16)
        WD = [WBIG[:, 0:4096], WBIG[:, 4096:8192]]
        WQS = [WU[0][:, 0:1024], WU[1][:, 0:1024]]
        WVS = [WU[0][:, 1024:2048], WU[1][:, 1024:2048]]
        WO = WBIG[:, 0:8192]
        W1V = WBIG[:, 0:8192]
        XNB0 = sb("XNB0", [128, D], BF16)
        XNB = [XNB0, XNB0]
        STG = [AT[:, 0:2048], AT[:, 2048:4096]]
        HBOF = 4096
        BQ0 = sb("BQ0", [128, 8], F32)
        BV0 = sb("BV0", [128, 8], F32)
        WAQ = sb("WAQ", [128, 24], F32)
        WAV = sb("WAV", [128, 24], F32)
        WQE = sb("WQE", [128, 24], F32)
        BO = SCR[0:1, 1024:2048]
        B1V = BO
        GW = sb("GW", [128, 8 * 16], BF16)
        BE = sb("BE", [128, 16], F32)
        B1QK = sb("B1QK", [128, 16], F32)
        B1Q32 = sb("B1Q32", [128, 8], F32)
        EPSB = sb("EPSB", [128, 1], F32)
        IDB = sb("IDB", [128, 128], BF16)
        ONR = sb("ONR", [1, 128], F32)
        ON4 = sb("ON4", [128, 4], F32)
        WSEL = sb("WSEL", [128, 16], F32)
        SCALL = sb("SCALL", [128, NTT * 16], F32)
        SC2 = sb("SC2", [128, 16], F32)
        GS = sb("GS", [128, 4], F32)
        GM = sb("GM", [128, 4], F32)
        SMASK = sb("SMASK", [128, 16], F32)
        SEL = sb("SEL", [128, 16], F32)
        RSC = sb("RSC", [128, 8], F32)
        COMB = sb("COMB", [128, NTT * 16], F32)
        SQT = SCR[:, 0:512]
        P = [pst(f"P{i}", [128, 512], F32) for i in range(4)]
        PT = [pst(f"PTp{i}", [128, 128], BF16) for i in range(2)]

        cg = Ch(nc, st, "cg")
        cy = Ch(nc, st, "cy")
        ct = Ch(nc, st, "ct")
        cv = Ch(nc, st, "cv")
        cs = Ch(nc, st, "cs")
        co = Ch(nc, st, "co")
        G, Y, T, V, S = nc.gpsimd, nc.sync, nc.tensor, nc.vector, nc.scalar
        pfv = {0: 0, 1: 0, 2: 0, 3: 0}
        pfs = {0: 0, 1: 0, 2: 0, 3: 0}

        for i in range(4):
            cy.inc(Y.dma_start(ACC[:, i * 4 * D:(i + 1) * 4 * D],
                               x_h[i * 512:(i + 1) * 512, :]
                               .rearrange("(n p) d -> p n d", p=128)), 16)
        for dst, srch in ((BQ0, bq0_h), (BV0, bv0_h), (WAQ, wvq_h), (WAV, wvv_h),
                          (BE, be_h), (B1QK, b1qk_h),
                          (ONR, onesr_h), (ON4, ones4_h), (WSEL, wsel_h)):
            cy.inc(Y.dma_start(dst[:], srch[:, :]), 16)
        t_loads = cy.n
        cg.inc(G.dma_start(IDB[:], idb_h[:, :]), 16)
        cg.inc(G.dma_start(GW[:], gw_h[:, :].rearrange("(k p) e -> p k e", p=128)), 16)
        t_gw = cg.n

        _w(V, (cy, t_loads))
        cv.inc(V.tensor_scalar_mul(WQE[:, 0:24:3], WAQ[:, 0:24:3], ISQ2 ** 9))
        cv.inc(V.tensor_scalar_mul(WQE[:, 1:24:3], WAQ[:, 1:24:3], ISQ2 ** 9))
        cv.inc(V.tensor_scalar_mul(WQE[:, 2:24:3], WAQ[:, 2:24:3], ISQ2 ** 6))
        cv.inc(V.tensor_scalar_mul(B1Q32[:], B1QK[:, 0:8], 1.0 / 32.0))
        cv.inc(V.memset(EPSB[:], 1e-5))
        t_prep = cv.n

        def layernorm_and_transpose(dep_v, dep_s):
            _w(S, (cv, dep_v), (cs, dep_s))
            _w(V, (cv, dep_v), (cs, dep_s))
            trans_tok = {}
            for i in range(NTT):
                a = ACC[:, i * D:(i + 1) * D]
                _w(S, (cv, cv.n))
                cs.inc(S.activation(SQT[:], a[:, 0:512], AF.Square))
                _w(V, (cs, cs.n))
                cv.inc(V.reduce_sum(RSC[:, 1:2], SQT[:], axis=AX.X))
                _w(S, (cv, cv.n))
                cs.inc(S.activation(SQT[:], a[:, 512:1024], AF.Square))
                _w(V, (cs, cs.n))
                cv.inc(V.reduce_sum(RSC[:, 2:3], SQT[:], axis=AX.X))
                cv.inc(V.reduce_sum(RSC[:, 0:1], a, axis=AX.X))
                _w(V, (cv, cv.n))
                cv.inc(V.tensor_add(RSC[:, 1:2], RSC[:, 1:2], RSC[:, 2:3]))
                _w(V, (cv, cv.n))
                cv.inc(V.tensor_scalar_mul(RSC[:, 0:1], RSC[:, 0:1], 1.0 / D))
                _w(V, (cv, cv.n))
                cv.inc(V.tensor_scalar_mul(RSC[:, 1:2], RSC[:, 1:2], 1.0 / D))
                cv.inc(V.scalar_tensor_tensor(RSC[:, 3:4], RSC[:, 0:1], -1.0,
                                              RSC[:, 0:1], ALU.mult, ALU.mult))
                _w(V, (cv, cv.n))
                cv.inc(V.tensor_add(RSC[:, 1:2], RSC[:, 1:2], RSC[:, 3:4]))
                _w(S, (cv, cv.n))
                cs.inc(S.activation(RSC[:, 4:5], RSC[:, 1:2], AF.Sqrt,
                                    bias=EPSB[:, 0:1]))
                _w(V, (cs, cs.n))
                cv.inc(V.reciprocal(RSC[:, 5:6], RSC[:, 4:5]))
                _w(V, (cv, cv.n))
                xb = XNB[i % 2]
                if i >= 1:
                    _w(V, (ct, trans_tok[i - 1]))
                t_x = cv.inc(V.tensor_scalar(xb[:], a, RSC[:, 0:1], RSC[:, 5:6],
                                             ALU.subtract, ALU.mult))
                ptfree = getattr(layernorm_and_transpose, "_ptfree", {0: 0, 1: 0})
                for k in range(ND):
                    _w(T, (cv, t_x), (cv, ptfree[k % 2]))
                    ct.inc(T.transpose(PT[k % 2][:], xb[:, k * 128:(k + 1) * 128], IDB[:]))
                    _w(V, (ct, ct.n))
                    ptfree[k % 2] = cv.inc(V.tensor_copy(
                        XNT[:, k * T_CORE + i * 128:k * T_CORE + (i + 1) * 128],
                        PT[k % 2][:]))
                layernorm_and_transpose._ptfree = ptfree
                trans_tok[i] = ct.n
            return cv.n

        t_ln0 = layernorm_and_transpose(t_prep, 0)

        qv_done = {}
        for e in range(ND):
            if e >= 2:
                _w(G, (ct, qv_done[e - 2]))
            else:
                _w(G, (cv, t_ln0))
            cg.inc(G.dma_start(WQS[e % 2],
                               wq0_h[:, e * 128:(e + 1) * 128]
                               .rearrange("(k p) c -> p k c", p=128)), 16)
            cg.inc(G.dma_start(WVS[e % 2],
                               wv0_h[:, e * 128:(e + 1) * 128]
                               .rearrange("(k p) c -> p k c", p=128)), 16)
            t_wl = cg.n
            _w(T, (cg, t_wl), (cv, t_ln0))

            def proj(wbuf, bias, banks):
                for tb in range(4):
                    nb = banks[tb % 2]
                    pb = P[nb]
                    _w(T, (cv, pfv[nb]), (cs, pfs[nb]))
                    for k in range(ND):
                        inst = T.matmul(pb[:], wbuf[:, k * 128:(k + 1) * 128],
                                        XNT[:, k * T_CORE + tb * 512:
                                            k * T_CORE + (tb + 1) * 512],
                                        start=(k == 0), stop=(k == ND - 1))
                    ct.inc(inst)
                    _w(V, (ct, ct.n))
                    pfv[nb] = cv.inc(V.tensor_scalar_add(
                        HF[:, 2048 + tb * 512:2048 + (tb + 1) * 512], pb[:],
                        bias[:, e:e + 1]))

            def wavedec(co):
                _w(V, (cv, cv.n))
                e0, o0 = HF[:, 2048:4096:2], HF[:, 2049:4096:2]
                cv.inc(V.tensor_sub(HF[:, co:co + 1024], e0, o0))
                cv.inc(V.tensor_add(SCR[:, 0:1024], e0, o0))
                _w(V, (cv, cv.n))
                e1, o1 = SCR[:, 0:1024:2], SCR[:, 1:1024:2]
                cv.inc(V.tensor_sub(HF[:, co + 1024:co + 1536], e1, o1))
                cv.inc(V.tensor_add(SCR[:, 1024:1536], e1, o1))
                _w(V, (cv, cv.n))
                e2, o2 = SCR[:, 1024:1536:2], SCR[:, 1025:1536:2]
                cv.inc(V.tensor_sub(HF[:, co + 1536:co + 1792], e2, o2))
                cv.inc(V.tensor_add(HF[:, co + 1792:co + 2048], e2, o2))

            proj(WQS[e % 2], BQ0, (0, 1))
            wavedec(0)
            proj(WVS[e % 2], BV0, (2, 3))
            wavedec(4096)
            cq, cvo = 0, 4096
            _w(V, (cv, cv.n))
            for (qo, vo, j, ln, po) in ((cq + 1792, cvo + 1792, 0, 256, 2048),
                                        (cq + 1536, cvo + 1536, 1, 256, 2304),
                                        (cq + 1024, cvo + 1024, 2, 512, 2560)):
                cv.inc(V.tensor_scalar_mul(SCR[:, 0:ln], HF[:, qo:qo + ln],
                                           WQE[:, e * 3 + j:e * 3 + j + 1]))
                cv.inc(V.tensor_scalar_mul(SCR[:, 1024:1024 + ln], HF[:, vo:vo + ln],
                                           WAV[:, e * 3 + j:e * 3 + j + 1]))
                _w(V, (cv, cv.n))
                cv.inc(V.tensor_mul(SCR[:, po:po + ln], SCR[:, 0:ln],
                                    SCR[:, 1024:1024 + ln]))
                _w(V, (cv, cv.n))
            cv.inc(V.tensor_scalar_mul(SCR[:, 0:1024], HF[:, cq:cq + 1024], ISQ2 ** 3))
            _w(V, (cv, cv.n))
            cv.inc(V.tensor_mul(SCR[:, 3072:4096], SCR[:, 0:1024],
                                HF[:, cvo:cvo + 1024]))
            _w(V, (cv, cv.n))
            cv.inc(V.tensor_add(SCR[:, 0:512:2], SCR[:, 2048:2304], SCR[:, 2304:2560]))
            cv.inc(V.tensor_sub(SCR[:, 1:512:2], SCR[:, 2048:2304], SCR[:, 2304:2560]))
            _w(V, (cv, cv.n))
            cv.inc(V.tensor_add(SCR[:, 512:1536:2], SCR[:, 0:512], SCR[:, 2560:3072]))
            cv.inc(V.tensor_sub(SCR[:, 513:1536:2], SCR[:, 0:512], SCR[:, 2560:3072]))
            _w(V, (cv, cv.n))
            at = AT[:, e * T_CORE:(e + 1) * T_CORE]
            cv.inc(V.tensor_add(at[:, 0:T_CORE:2], SCR[:, 512:1536], SCR[:, 3072:4096]))
            cv.inc(V.tensor_sub(at[:, 1:T_CORE:2], SCR[:, 512:1536], SCR[:, 3072:4096]))
            qv_done[e] = ct.n
        t_at = cv.n

        cg.inc(G.dma_start(WO[:], wo_h[:, :].rearrange("(k p) e -> p k e", p=128)), 16)
        t_wo = cg.n
        _w(Y, (cv, t_at), (cs, cs.n))
        cy.inc(Y.dma_start(BO, bo_h[:, :]), 16)
        t_bo = cy.n

        _w(T, (cg, t_wo), (cv, t_at), (cy, t_bo))
        for i in range(NTT):
            for dh in range(2):
                nb = (2 * i + dh) % 2
                pb = P[nb]
                _w(T, (cv, pfv[nb]), (cs, pfs[nb]))
                for k in range(ND):
                    T.matmul(pb[:], AT[:, k * T_CORE + i * 128:k * T_CORE + (i + 1) * 128],
                             WO[:, k * D + dh * 512:k * D + (dh + 1) * 512],
                             start=(k == 0), stop=False)
                ct.inc(T.matmul(pb[:], ONR[:, :], BO[0:1, dh * 512:(dh + 1) * 512],
                                start=False, stop=True))
                _w(V, (ct, ct.n))
                sl = ACC[:, i * D + dh * 512:i * D + (dh + 1) * 512]
                pfv[nb] = cv.inc(V.tensor_tensor(sl, sl, pb[:], ALU.add))
        t_attn = cv.n

        _w(T, (cg, t_gw))
        scall_tok = {}
        for i in range(NTT):
            _w(T, (cs, pfs[2]))
            for k in range(ND):
                inst = T.matmul(P[2][:, 0:16],
                                XNT[:, k * T_CORE + i * 128:k * T_CORE + (i + 1) * 128],
                                GW[:, k * 16:(k + 1) * 16],
                                start=(k == 0), stop=(k == ND - 1))
            ct.inc(inst)
            _w(S, (ct, ct.n))
            pfs[2] = cs.inc(S.activation(SCALL[:, i * 16:(i + 1) * 16],
                                         P[2][:, 0:16], AF.Sigmoid))
            scall_tok[i] = cs.n
        comb_tok = {}
        for i in range(NTT):
            SCORES = SCALL[:, i * 16:(i + 1) * 16]
            _w(V, (cs, scall_tok[i]), (cv, cv.n))
            cv.inc(V.tensor_add(SC2[:], SCORES[:], BE[:]))
            for g in range(4):
                _w(V, (cv, cv.n))
                sl = SC2[:, 4 * g:4 * g + 4]
                cv.inc(V.reduce_max(RSC[:, 0:1], sl, axis=AX.X))
                _w(V, (cv, cv.n))
                cv.inc(V.tensor_scalar(SMASK[:, 0:4], sl, RSC[:, 0:1], None,
                                       ALU.is_equal))
                _w(V, (cv, cv.n))
                cv.inc(V.scalar_tensor_tensor(SMASK[:, 4:8], SMASK[:, 0:4], -BIG, sl,
                                              ALU.mult, ALU.add))
                _w(V, (cv, cv.n))
                cv.inc(V.reduce_max(RSC[:, 1:2], SMASK[:, 4:8], axis=AX.X))
                _w(V, (cv, cv.n))
                cv.inc(V.tensor_add(GS[:, g:g + 1], RSC[:, 0:1], RSC[:, 1:2]))
            _w(V, (cv, cv.n))
            cv.inc(V.reduce_max(RSC[:, 0:1], GS[:], axis=AX.X))
            _w(V, (cv, cv.n))
            cv.inc(V.tensor_scalar(GM[:], GS[:], RSC[:, 0:1], None, ALU.is_equal))
            _w(V, (cv, cv.n))
            cv.inc(V.scalar_tensor_tensor(GM[:], GM[:], -BIG, GS[:], ALU.mult, ALU.add))
            _w(V, (cv, cv.n))
            cv.inc(V.reduce_max(RSC[:, 1:2], GM[:], axis=AX.X))
            _w(V, (cv, cv.n))
            cv.inc(V.tensor_scalar(GM[:], GS[:], RSC[:, 1:2], None, ALU.is_ge))
            _w(V, (cv, cv.n))
            for g in range(4):
                cv.inc(V.tensor_scalar_mul(SMASK[:, 4 * g:4 * g + 4], ON4[:],
                                           GM[:, g:g + 1]))
            _w(V, (cv, cv.n))
            cv.inc(V.tensor_mul(SC2[:], SC2[:], SMASK[:]))
            cv.inc(V.memset(SEL[:], 0.0))
            for it in range(4):
                _w(V, (cv, cv.n))
                cv.inc(V.reduce_max(RSC[:, 0:1], SC2[:], axis=AX.X))
                _w(V, (cv, cv.n))
                cv.inc(V.tensor_scalar(SMASK[:], SC2[:], RSC[:, 0:1], None,
                                       ALU.is_equal))
                _w(V, (cv, cv.n))
                cv.inc(V.tensor_mul(SMASK[:], SMASK[:], WSEL[:]))
                _w(V, (cv, cv.n))
                cv.inc(V.reduce_max(RSC[:, 1:2], SMASK[:], axis=AX.X))
                _w(V, (cv, cv.n))
                cv.inc(V.tensor_scalar(SMASK[:], SMASK[:], RSC[:, 1:2], None,
                                       ALU.is_equal))
                _w(V, (cv, cv.n))
                cv.inc(V.tensor_add(SEL[:], SEL[:], SMASK[:]))
                cv.inc(V.scalar_tensor_tensor(SC2[:], SMASK[:], -BIG, SC2[:],
                                              ALU.mult, ALU.add))
            _w(V, (cv, cv.n))
            cv.inc(V.tensor_mul(SEL[:], SEL[:], SCORES[:]))
            _w(V, (cv, cv.n))
            cv.inc(V.reduce_sum(RSC[:, 0:1], SEL[:], axis=AX.X))
            _w(V, (cv, cv.n))
            cv.inc(V.tensor_scalar_add(RSC[:, 0:1], RSC[:, 0:1], 1e-20))
            _w(V, (cv, cv.n))
            cv.inc(V.reciprocal(RSC[:, 1:2], RSC[:, 0:1]))
            _w(V, (cv, cv.n))
            comb_tok[i] = cv.inc(V.tensor_scalar(COMB[:, i * 16:(i + 1) * 16],
                                                 SEL[:], RSC[:, 1:2],
                                                 2.5, ALU.mult, ALU.mult))
        t_comb = cv.n

        moe_done = {}
        for e in range(20):
            shared = e >= 16
            c = e - 16
            if e >= 2:
                _w(G, (ct, moe_done[e - 2]))
            else:
                _w(G, (ct, ct.n))
            if shared:
                cg.inc(G.dma_start(WU[e % 2][:],
                                   sup_h[:, c * 512:(c + 1) * 512]
                                   .rearrange("(k p) f -> p k f", p=128)), 16)
                cg.inc(G.dma_start(WD[e % 2],
                                   sdn_h[c * 512:(c + 1) * 512, :]
                                   .rearrange("(k p) d -> p k d", p=128)), 16)
            else:
                cg.inc(G.dma_start(WU[e % 2][:],
                                   up_h[e, :, :].rearrange("(k p) f -> p k f", p=128)), 16)
                cg.inc(G.dma_start(WD[e % 2],
                                   dn_h[e, :, :].rearrange("(k p) d -> p k d", p=128)), 16)
            t_wl = cg.n
            _w(T, (cg, t_wl), (cv, t_attn))
            for ft in range(4):
                for tb in range(4):
                    nb = (ft * 4 + tb) % 2
                    pb = P[nb]
                    _w(T, (cs, pfs[nb]), (cv, pfv[nb]))
                    for k in range(ND):
                        inst = T.matmul(pb[:],
                                        WU[e % 2][:, k * 512 + ft * 128:
                                                  k * 512 + (ft + 1) * 128],
                                        XNT[:, k * T_CORE + tb * 512:
                                            k * T_CORE + (tb + 1) * 512],
                                        start=(k == 0), stop=(k == ND - 1))
                    ct.inc(inst)
                    _w(S, (ct, ct.n))
                    if ft < 3:
                        sl = HF[:, ft * T_CORE + tb * 512:ft * T_CORE + (tb + 1) * 512]
                    else:
                        sl = SCR[:, tb * 512:(tb + 1) * 512]
                    cs.inc(S.activation(sl, pb[:], AF.Relu))
                    _w(S, (cs, cs.n))
                    pfs[nb] = cs.inc(
                        S.activation(AT[:, HBOF + ft * T_CORE + tb * 512:
                                        HBOF + ft * T_CORE + (tb + 1) * 512],
                                     sl, AF.Square))
            t_h = cs.n
            _w(T, (cs, t_h))
            for i in range(NTT):
                for dh in range(2):
                    nb = 2 + (2 * i + dh) % 2
                    pb = P[nb]
                    _w(T, (cv, pfv[nb]), (cs, pfs[nb]))
                    for ft in range(4):
                        inst = T.matmul(pb[:],
                                        AT[:, HBOF + ft * T_CORE + i * 128:
                                           HBOF + ft * T_CORE + (i + 1) * 128],
                                        WD[e % 2][:, ft * D + dh * 512:
                                                  ft * D + (dh + 1) * 512],
                                        start=(ft == 0), stop=(ft == 3))
                    ct.inc(inst)
                    _w(V, (ct, ct.n))
                    if not shared:
                        _w(V, (cv, comb_tok[i]))
                    sl = ACC[:, i * D + dh * 512:i * D + (dh + 1) * 512]
                    if shared:
                        pfv[nb] = cv.inc(V.tensor_tensor(sl, sl, pb[:], ALU.add))
                    else:
                        pfv[nb] = cv.inc(V.scalar_tensor_tensor(
                            sl, pb[:], COMB[:, i * 16 + e:i * 16 + e + 1], sl,
                            ALU.mult, ALU.add))
            moe_done[e] = ct.n
        t_moe = cv.n

        _w(G, (cv, t_moe))
        for i in range(4):
            co.inc(G.dma_start(out1_h[i * 512:(i + 1) * 512, :]
                               .rearrange("(n p) d -> p n d", p=128),
                               ACC[:, i * 4 * D:(i + 1) * 4 * D]), 16)
        t_ln1 = layernorm_and_transpose(t_moe, 0)
        _w(G, (cv, t_ln1))
        co.inc(G.dma_start(xn1_h[:, :].rearrange("(k p) t -> p k t", p=128),
                           XNT[:]), 16)

        _w(G, (ct, ct.n))
        cg.inc(G.dma_start(W1V,
                           w1_h[:, 2048:3072].rearrange("(k p) e -> p k e", p=128)), 16)
        _w(Y, (ct, ct.n), (cs, cs.n))
        cy.inc(Y.dma_start(B1V[:], b1v_h[:, :]), 16)
        t_w1v = cg.n
        for which in range(2):
            cofs = 0 if which == 0 else 1024
            out_hbm = qs_h if which == 0 else kt_h
            for e in range(ND):
                _w(G, (ct, ct.n))
                cg.inc(G.dma_start(WQS[e % 2],
                                   w1_h[:, cofs + e * 128:cofs + (e + 1) * 128]
                                   .rearrange("(k p) c -> p k c", p=128)), 16)
                _w(T, (cg, cg.n), (cv, t_ln1), (co, co.n))
                stg = STG[e % 2]
                for tb in range(4):
                    nb = tb % 2
                    pb = P[nb]
                    _w(T, (cs, pfs[nb]), (cv, pfv[nb]))
                    for k in range(ND):
                        inst = T.matmul(pb[:], WQS[e % 2][:, k * 128:(k + 1) * 128],
                                        XNT[:, k * T_CORE + tb * 512:
                                            k * T_CORE + (tb + 1) * 512],
                                        start=(k == 0), stop=(k == ND - 1))
                    ct.inc(inst)
                    _w(S, (ct, ct.n))
                    if which == 0:
                        pfs[nb] = cs.inc(S.activation(stg[:, tb * 512:(tb + 1) * 512],
                                            pb[:], AF.Identity, bias=B1Q32[:, e:e + 1],
                                            scale=1.0 / 32.0))
                    else:
                        pfs[nb] = cs.inc(S.activation(stg[:, tb * 512:(tb + 1) * 512],
                                            pb[:], AF.Identity,
                                            bias=B1QK[:, 8 + e:8 + e + 1]))
                _w(G, (cs, cs.n))
                co.inc(G.dma_start(out_hbm[e * 128:(e + 1) * 128, :], stg[:]), 16)
        _w(T, (cg, t_w1v))
        for i in range(NTT):
            stg = STG[i % 2]
            _w(T, (co, co.n))
            for eh in range(2):
                nb = (2 * i + eh) % 2
                pb = P[nb]
                _w(T, (cv, pfv[nb]), (cs, pfs[nb]))
                for k in range(ND):
                    T.matmul(pb[:], XNT[:, k * T_CORE + i * 128:k * T_CORE + (i + 1) * 128],
                             W1V[:, k * D + eh * 512:k * D + (eh + 1) * 512],
                             start=(k == 0), stop=False)
                ct.inc(T.matmul(pb[:], ONR[:, :], B1V[0:1, eh * 512:(eh + 1) * 512],
                                start=False, stop=True))
                _w(V, (ct, ct.n))
                pfv[nb] = cv.inc(V.tensor_copy(stg[:, eh * 512:(eh + 1) * 512], pb[:]))
            _w(G, (cv, cv.n))
            co.inc(G.dma_start(vo_h[i * 128:(i + 1) * 128, :], stg[:, 0:1024]), 16)
        G.wait_ge(co.h, co.n)
    return nc



NKB_FIX = [32, 32, 64, 64]
SEG_OFF = [0, 32, 64, 128]


def build_B(attn_only=False):
    nc = bass.Bass("TRN2", target_bir_lowering=False)
    dt = nc.dram_tensor
    if attn_only:
        dbg_negc = dt("dbg_negc", [1, 512], F32, kind="ExternalOutput")
        dbg_lr = dt("dbg_lr", [1, 4], F32, kind="ExternalOutput")
        dbg_yp = dt("dbg_yp", [128, 4096], F32, kind="ExternalOutput")
        dbg_pts = dt("dbg_pts", [128, 4096], BF16, kind="ExternalOutput")
    out1_h = dt("out1", [T_CORE, D], F32, kind="ExternalInput")
    xn1_h = dt("xn1T", [D, T_CORE], BF16, kind="ExternalInput")
    qs_h = dt("qsT", [D, T_CORE], BF16, kind="ExternalInput")
    ktf_h = dt("ktG", [D, 192 * 128], BF16, kind="ExternalInput")
    vf_h = dt("vG", [192 * 128, D], BF16, kind="ExternalInput")
    vmask_h = dt("vmask", [128, 192], F32, kind="ExternalInput")
    g1_h = dt("g1T", [D, 15], F32, kind="ExternalInput")
    f1w_h = dt("f1T", [15, D, 2048], F32, kind="ExternalInput")
    f1b_h = dt("f1b", [15, 128, 16], F32, kind="ExternalInput")
    geg_h = dt("geg", [15, 128, 8], F32, kind="ExternalInput")
    f2w_h = dt("f2T", [15, D, D], F32, kind="ExternalInput")
    fba_h = dt("fbaug", [16, D], F32, kind="ExternalInput")
    s1w_h = dt("s1P", [16, D, 256], F32, kind="ExternalInput")
    s1b_h = dt("s1b", [128, 32], F32, kind="ExternalInput")
    sgg_h = dt("sgeg", [128, 16], F32, kind="ExternalInput")
    s2w_h = dt("s2T", [2048, D], F32, kind="ExternalInput")
    msk_h = dt("masks", [128, 2048], F32, kind="ExternalInput")
    idb_h = dt("idb", [128, 128], F32, kind="ExternalInput")
    onesr_h = dt("onesr", [1, 128], F32, kind="ExternalInput")
    ones4_h = dt("ones4", [128, 4], F32, kind="ExternalInput")
    out_h = dt("out", [T_CORE, D], F32, kind="ExternalOutput")

    tile_base = [0, 4, 8, 12]

    st = ExitStack()
    with st:
        sb = lambda n, s, d_: st.enter_context(nc.sbuf_tensor(n, s, d_))
        pst = lambda n, s, d_: st.enter_context(nc.psum_tensor(n, s, d_))
        ACC = sb("ACC", [128, NTT * D], F32)
        XNT = sb("XNT", [128, ND * T_CORE], BF16)
        AR1 = sb("AR1", [128, 16384], BF16)
        AR2 = sb("AR2", [128, 16384], BF16)
        AR3 = sb("AR3", [128, 8192], BF16)
        AR4 = sb("AR4", [128, 7168], F32)
        IDF = sb("IDF", [128, 128], F32)
        ONR = sb("ONR", [1, 128], F32)
        ON4 = sb("ON4", [128, 4], F32)
        ONB = sb("ONB", [1, 128], BF16)
        VMB = sb("VMB", [128, 192], BF16)
        NEGCB = sb("NEGCB", [1, 512], BF16)
        LC = sb("LC", [128, 1], F32)
        RI = sb("RI", [128, 4], F32)
        RSC = sb("RSC", [128, 8], F32)
        ESC = sb("ESC", [128, 15], F32)
        SELB = sb("SELB", [128, 15], F32)
        TMP15 = sb("TMP15", [128, 15], F32)
        F1B = AR4[:, 5120:5136]
        GEG = AR4[:, 5136:5144]
        S1B = AR4[:, 5152:5184]
        SGG = AR4[:, 5184:5200]
        G1W = sb("G1W", [128, 8 * 15], BF16)
        P = [pst(f"P{i}", [128, 512], F32) for i in range(4)]
        PLL = pst("PLL", [1, 512], F32)
        PLC = pst("PLC", [128, 4], F32)
        PCT = pst("PCT", [16, 128], F32)

        KC = [AR1[:, 0:8192], AR1[:, 8192:16384]]
        VC = [AR2[:, 0:8192], AR2[:, 8192:16384]]
        QSG = AR3[:, 0:4096]
        PTS = AR3[:, 4096:8192]
        YP = AR4[:, 0:4096]
        MSK = AR4[:, 4096:6144]
        TMS = AR4[:, 6144:6656]
        SQ = AR4[:, 6144:6656]
        LR = AR4[0:1, 6656:7168]
        NQR = AR4[32:33, 6656:7168]
        NEGC = AR4[64:65, 6656:7168]
        MKR = AR4[0:1, 6144:6656]

        COMB1 = AR4[:, 1536:1792]
        cg = Ch(nc, st, "cg")
        cy = Ch(nc, st, "cy")
        ct = Ch(nc, st, "ct")
        cv = Ch(nc, st, "cv")
        cs = Ch(nc, st, "cs")
        co = Ch(nc, st, "co")
        G, Y, T, V, S = nc.gpsimd, nc.sync, nc.tensor, nc.vector, nc.scalar

        for i in range(4):
            cy.inc(Y.dma_start(ACC[:, i * 4 * D:(i + 1) * 4 * D],
                               out1_h[i * 512:(i + 1) * 512, :]
                               .rearrange("(n p) d -> p n d", p=128)), 16)
        cy.inc(Y.dma_start(XNT[:], xn1_h[:, :].rearrange("(k p) t -> p k t", p=128)), 16)
        cy.inc(Y.dma_start(MSK[:], msk_h[:, :]), 16)
        cy.inc(Y.dma_start(IDF[:], idb_h[:, :]), 16)
        cy.inc(Y.dma_start(ONR[:], onesr_h[:, :]), 16)
        cy.inc(Y.dma_start(ON4[:], ones4_h[:, :]), 16)
        t_loads = cy.n
        cg.inc(G.dma_start(VMB[:], vmask_h[:, :]), 16)
        t_vmb = cg.n
        _w(V, (cy, t_loads))
        cv.inc(V.tensor_copy(ONB[:], ONR[:, :]))
        t_ones = cv.n

        nchunks_max = 8
        for c in range(nchunks_max):
            _w(Y, (ct, ct.n))
            cy.inc(Y.dma_start(KC[c % 2][:],
                               ktf_h[:, 128 * 128 + c * 1024:128 * 128 + (c + 1) * 1024]
                               .rearrange("(k p) t -> p k t", p=128)), 16)
            t_kc = cy.n
            for half in range(2):
                pbap = PLL[:] if half == 0 else P[2][0:1, 0:512]
                for k in range(ND):
                    _w(V, (cy, t_kc), (ct, ct.n), (cv, cv.n))
                    sl = KC[c % 2][:, k * 1024 + half * 512:k * 1024 + (half + 1) * 512]
                    cv.inc(V.tensor_mul(SQ[:], sl, sl))
                    _w(T, (cv, cv.n))
                    ct.inc(T.matmul(pbap, ON4[:, 0:1], SQ[:],
                                    start=(k == 0), stop=(k == ND - 1),
                                    skip_group_check=True))
            _w(V, (ct, ct.n))
            cv.inc(V.tensor_copy(MKR[0:1, 0:512], PLL[:]))
            _w(V, (cv, cv.n))
            cv.inc(V.tensor_tensor(MKR[0:1, 0:512], MKR[0:1, 0:512],
                                   P[2][0:1, 0:512], ALU.max))
            _w(V, (cv, cv.n))
            cv.inc(V.reduce_max(RSC[0:1, 0:1], MKR[0:1, 0:512], axis=AX.X))
            _w(V, (cv, cv.n))
            if c == 0:
                cv.inc(V.tensor_copy(RSC[0:1, 1:2], RSC[0:1, 0:1]))
            else:
                cv.inc(V.tensor_tensor(RSC[0:1, 1:2], RSC[0:1, 1:2], RSC[0:1, 0:1],
                                       ALU.max))
        _w(S, (cv, cv.n))
        cs.inc(S.activation(RSC[0:1, 2:3], RSC[0:1, 1:2], AF.Sqrt))
        t_mk = cs.n

        _w(T, (cg, t_vmb))
        yfree = {}
        stfree = {}
        for g in range(4):
            nkb = NKB_FIX[g]
            nch = nkb // 8
            segc = SEG_OFF[g] * 128
            _w(Y, (ct, ct.n))
            cy.inc(Y.dma_start(QSG[:],
                               qs_h[:, g * 512:(g + 1) * 512]
                               .rearrange("(k p) t -> p k t", p=128)), 16)
            t_qs = cy.n
            for k in range(ND):
                _w(V, (cy, t_qs), (ct, ct.n), (cv, cv.n))
                sl = QSG[:, k * 512:(k + 1) * 512]
                cv.inc(V.tensor_mul(SQ[:], sl, sl))
                _w(T, (cv, cv.n))
                ct.inc(T.matmul(PLL[:], ON4[:, 0:1], SQ[:],
                                start=(k == 0), stop=(k == ND - 1),
                                skip_group_check=True))
            _w(V, (ct, ct.n))
            cv.inc(V.tensor_copy(NQR[0:1, 0:512], PLL[:]))
            _w(S, (cv, cv.n), (cs, t_mk))
            cs.inc(S.activation(NQR[0:1, 0:512], NQR[0:1, 0:512], AF.Sqrt))
            _w(V, (cs, cs.n))
            cv.inc(V.tensor_scalar(NEGC[0:1, 0:512], NQR[0:1, 0:512],
                                   RSC[0:1, 2:3], -1.01, ALU.mult, ALU.mult))
            _w(V, (cv, cv.n))
            t_negc = cv.inc(V.tensor_copy(NEGCB[:], NEGC[0:1, 0:512]))
            y_evac_tok = cv.n
            pt_done = {}
            ymm_done = {}
            t_g0 = ct.n
            for c in range(nch):
                _w(Y, (ct, ymm_done.get(c - 2, t_g0)))
                cy.inc(Y.dma_start(KC[c % 2][:],
                                   ktf_h[:, segc + c * 1024:segc + (c + 1) * 1024]
                                   .rearrange("(k p) t -> p k t", p=128)), 16)
                cy.inc(Y.dma_start(VC[c % 2][:],
                                   vf_h[SEG_OFF[g] * 128 + c * 1024:
                                        SEG_OFF[g] * 128 + (c + 1) * 1024, :]
                                   .rearrange("(k p) d -> p k d", p=128)), 16)
                t_cld = cy.n
                kbs = list(range(8))
                for kb in kbs:
                    kbg = 8 * c + kb
                    pb = P[kb % 2]
                    _w(T, (cs, stfree.get(kb % 2, 0)), (cy, t_cld), (cv, t_negc))
                    for k in range(ND):
                        T.matmul(pb[:], KC[c % 2][:, k * 1024 + kb * 128:
                                                  k * 1024 + (kb + 1) * 128],
                                 QSG[:, k * 512:(k + 1) * 512],
                                 start=(k == 0), stop=False)
                    ct.inc(T.matmul(pb[:], ONB[:, :], NEGCB[0:1, :],
                                    start=False, stop=True))
                    st_tok = ct.n
                    dst = PTS[:, kb * 512:(kb + 1) * 512]
                    if kbg >= nkb - 4:
                        m = kbg - (nkb - 4)
                        _w(V, (ct, st_tok), (cs, cs.n))
                        cv.inc(V.tensor_tensor(TMS[:], pb[:], MSK[:, m * 512:
                                                                  (m + 1) * 512],
                                               ALU.add))
                        _w(S, (cv, cv.n), (ct, ymm_done.get(c - 1, 0)))
                        cs.inc(S.activation(dst, TMS[:], AF.Exp))
                    else:
                        _w(S, (ct, st_tok), (ct, ymm_done.get(c - 1, 0)))
                        cs.inc(S.activation(dst, pb[:], AF.Exp))
                    stfree[kb % 2] = cs.n
                    pt_done[kbg] = cs.n
                    _w(T, (cs, pt_done[kbg]))
                    for sub in range(4):
                        inst = T.matmul(PLC[:, sub:sub + 1],
                                        PTS[:, kb * 512 + sub * 128:
                                            kb * 512 + (sub + 1) * 128],
                                        VMB[:, SEG_OFF[g] + kbg:SEG_OFF[g] + kbg + 1],
                                        start=(kbg == 0 and sub == 0),
                                        stop=(kbg == nkb - 1 and sub == 3),
                                        skip_group_check=True)
                    ct.inc(inst)
                for sub in range(4):
                    for dh in range(2):
                        nb = 2 + (sub * 2 + dh) % 2
                        pb = P[nb]
                        _w(T, (cv, yfree.get(nb, 0)))
                        for n_i, kb in enumerate(kbs):
                            inst = T.matmul(pb[:], PTS[:, kb * 512 + sub * 128:
                                              kb * 512 + (sub + 1) * 128],
                                     VC[c % 2][:, kb * 1024 + dh * 512:
                                               kb * 1024 + (dh + 1) * 512],
                                     start=(n_i == 0), stop=(n_i == len(kbs) - 1))
                        ct.inc(inst)
                        _w(V, (ct, ct.n))
                        dsty = YP[:, sub * 1024 + dh * 512:sub * 1024 + (dh + 1) * 512]
                        if c == 0:
                            yfree[nb] = cv.inc(V.tensor_copy(dsty, pb[:]))
                        else:
                            yfree[nb] = cv.inc(V.tensor_tensor(dsty, dsty, pb[:],
                                                               ALU.add))
                        y_evac_tok = cv.n
                ymm_done[c] = ct.n
            _w(V, (ct, ct.n))
            for sub in range(4):
                cv.inc(V.reciprocal(RI[:, sub:sub + 1], PLC[:, sub:sub + 1]))
            for sub in range(4):
                _w(V, (cv, cv.n))
                tt = tile_base[g] + sub
                sl = ACC[:, tt * D:(tt + 1) * D]
                cv.inc(V.scalar_tensor_tensor(sl, YP[:, sub * 1024:(sub + 1) * 1024],
                                              RI[:, sub:sub + 1], sl,
                                              ALU.mult, ALU.add))
        t_attn_done_v = cv.n
        t_attn_done_t = ct.n
        t_attn_done_s = cs.n
        if attn_only:
            _w(G, (cv, cv.n))
            co.inc(G.dma_start(dbg_negc[:, :], NEGC), 16)
            co.inc(G.dma_start(dbg_lr[:, :], RI[0:1, 0:4]), 16)
            co.inc(G.dma_start(dbg_yp[:, :], YP), 16)
            co.inc(G.dma_start(dbg_pts[:, :], PTS), 16)
            for i in range(4):
                co.inc(G.dma_start(out_h[i * 512:(i + 1) * 512, :]
                                   .rearrange("(n p) d -> p n d", p=128),
                                   ACC[:, i * 4 * D:(i + 1) * 4 * D]), 16)
            G.wait_ge(co.h, co.n)
            return nc

        cg.inc(G.dma_start(G1W[:], g1_h[:, :].rearrange("(k p) e -> p k e", p=128)), 16)
        t_g1 = cg.n
        for i in range(NTT):
            _w(T, (cg, t_g1), (cs, cs.n), (cv, cv.n))
            for k in range(ND):
                inst = T.matmul(P[0][:, 0:15],
                                XNT[:, k * T_CORE + i * 128:k * T_CORE + (i + 1) * 128],
                                G1W[:, k * 15:(k + 1) * 15],
                                start=(k == 0), stop=(k == ND - 1))
            ct.inc(inst)
            _w(V, (ct, ct.n))
            cv.inc(V.reduce_max(RSC[:, 0:1], P[0][:, 0:15], axis=AX.X, negate=True))
            _w(S, (cv, cv.n))
            cs.inc(S.activation(ESC[:], P[0][:, 0:15], AF.Exp, bias=RSC[:, 0:1]))
            _w(V, (cs, cs.n))
            cv.inc(V.tensor_copy(SELB[:], ESC[:]))
            for it in range(3):
                _w(V, (cv, cv.n))
                cv.inc(V.reduce_max(RSC[:, 1:2], SELB[:], axis=AX.X))
                if it < 2:
                    _w(V, (cv, cv.n))
                    cv.inc(V.tensor_scalar(TMP15[:], SELB[:], RSC[:, 1:2], None,
                                           ALU.is_equal))
                    _w(V, (cv, cv.n))
                    cv.inc(V.scalar_tensor_tensor(SELB[:], TMP15[:], -BIG, SELB[:],
                                                  ALU.mult, ALU.add))
            _w(V, (cv, cv.n))
            cv.inc(V.tensor_scalar(TMP15[:], ESC[:], RSC[:, 1:2], None, ALU.is_ge))
            _w(V, (cv, cv.n))
            cv.inc(V.tensor_mul(TMP15[:], TMP15[:], ESC[:]))
            _w(V, (cv, cv.n))
            cv.inc(V.reduce_sum(RSC[:, 2:3], TMP15[:], axis=AX.X))
            _w(V, (cv, cv.n))
            cv.inc(V.tensor_scalar_add(RSC[:, 2:3], RSC[:, 2:3], 1e-20))
            _w(V, (cv, cv.n))
            cv.inc(V.reciprocal(RSC[:, 3:4], RSC[:, 2:3]))
            _w(V, (cv, cv.n))
            cv.inc(V.tensor_scalar_mul(COMB1[:, i * 16:i * 16 + 15], TMP15[:],
                                       RSC[:, 3:4]))
            cv.inc(V.memset(COMB1[:, i * 16 + 15:i * 16 + 16], 1.0))
            _w(T, (cv, cv.n))
            ct.inc(T.transpose(PCT[:], COMB1[:, i * 16:(i + 1) * 16], IDF[:]))
            _w(V, (ct, ct.n))
            cv.inc(V.tensor_copy(AR4[0:16, 2048 + i * 128:2048 + (i + 1) * 128],
                                 PCT[:]))
        t_comb1 = cv.n
        CT = AR4[0:16, 2048:4096]
        FB = AR4[0:16, 4096:5120]
        HP = AR4[:, 0:1024]
        GL = AR4[:, 1024:1536]
        FW = AR1[:, 0:16384]
        FC2 = AR2[:, 0:8192]
        AB = AR2[:, 8192:12288]
        SF2 = AR1[:, 0:16384]
        SAB = AR2[:, 0:8192]
        SW = [AR2[:, 8192 + i * 2048:8192 + (i + 1) * 2048] for i in range(2)]

        _w(Y, (cv, t_attn_done_v), (cs, t_attn_done_s))
        cy.inc(Y.dma_start(AR4[0:16, 4096:5120], fba_h[:, :]), 16)
        t_fb = cy.n
        for i in range(NTT):
            for dh in range(2):
                pb = P[1 + (2 * i + dh) % 2]
                _w(T, (cv, t_comb1), (cy, t_fb), (cv, cv.n))
                ct.inc(T.matmul(pb[:], CT[:, i * 128:(i + 1) * 128],
                                FB[:, dh * 512:(dh + 1) * 512],
                                start=True, stop=True))
                _w(V, (ct, ct.n))
                sl = ACC[:, i * D + dh * 512:i * D + (dh + 1) * 512]
                cv.inc(V.tensor_tensor(sl, sl, pb[:], ALU.add))

        ffs = {nb: cs.n for nb in range(4)}
        ffv = {nb: cv.n for nb in range(4)}
        fc2_done = {}
        for e in range(15):
            _w(G, (ct, ct.n))
            cg.inc(G.dma_start(FW, f1w_h[e, :, :]
                               .rearrange("(k p) f -> p k f", p=128)), 16)
            cg.inc(G.dma_start(FC2, f2w_h[e, :, :]
                               .rearrange("(k p) d -> p k d", p=128)), 16)
            _w(Y, (cs, cs.n))
            cy.inc(Y.dma_start(F1B[:], f1b_h[e, :, :]), 16)
            cy.inc(Y.dma_start(GEG[:], geg_h[e, :, :]), 16)
            t_wl, t_bl = cg.n, cy.n
            for tc in range(4):
                t_abw = fc2_done.get(tc - 1, 0)
                for fp in range(8):
                    for half in range(2):
                        nb = (fp * 2 + half) % 2
                        pb = P[nb]
                        fti = fp + 8 * half
                        _w(T, (cg, t_wl), (cs, ffs.get(nb, 0)))
                        for k in range(ND):
                            inst = T.matmul(pb[:],
                                            FW[:, k * 2048 + fti * 128:
                                               k * 2048 + (fti + 1) * 128],
                                            XNT[:, k * T_CORE + tc * 512:
                                                k * T_CORE + (tc + 1) * 512],
                                            start=(k == 0), stop=(k == ND - 1))
                        ct.inc(inst)
                        _w(S, (ct, ct.n), (cy, t_bl))
                        ffs[nb] = cs.inc(S.activation(
                            HP[:, half * 512:(half + 1) * 512], pb[:],
                            AF.Identity, bias=F1B[:, fti:fti + 1]))
                    _w(S, (cs, cs.n))
                    cs.inc(S.activation(GL[:], HP[:, 512:1024], AF.Gelu))
                    _w(V, (cs, cs.n), (ct, t_abw))
                    cv.inc(V.scalar_tensor_tensor(AB[:, fp * 512:(fp + 1) * 512],
                                                  GL[:], GEG[:, fp:fp + 1],
                                                  HP[:, 0:512], ALU.mult, ALU.mult))
                t_a = cv.n
                _w(T, (cv, t_a))
                for ts4 in range(4):
                    for dh in range(2):
                        nb = 2 + (ts4 * 2 + dh) % 2
                        pb = P[nb]
                        _w(T, (cv, ffv.get(nb, 0)))
                        for ft in range(8):
                            inst = T.matmul(pb[:],
                                            AB[:, ft * 512 + ts4 * 128:
                                               ft * 512 + (ts4 + 1) * 128],
                                            FC2[:, ft * D + dh * 512:
                                                ft * D + (dh + 1) * 512],
                                            start=(ft == 0), stop=(ft == 7))
                        ct.inc(inst)
                        _w(V, (ct, ct.n))
                        i = tc * 4 + ts4
                        sl = ACC[:, i * D + dh * 512:i * D + (dh + 1) * 512]
                        ffv[nb] = cv.inc(V.scalar_tensor_tensor(
                            sl, pb[:], COMB1[:, i * 16 + e:i * 16 + e + 1], sl,
                            ALU.mult, ALU.add))
                fc2_done[tc] = ct.n

        _w(G, (ct, ct.n))
        cg.inc(G.dma_start(SF2, s2w_h[:, :].rearrange("(k p) d -> p k d", p=128)), 16)
        cy.inc(Y.dma_start(S1B[:], s1b_h[:, :]), 16)
        cy.inc(Y.dma_start(SGG[:], sgg_h[:, :]), 16)
        t_sw, t_sb = cg.n, cy.n
        sfc2_done = {}
        for tc in range(4):
            t_abw = sfc2_done.get(tc - 1, 0)
            for i in range(16):
                _w(G, (ct, ct.n))
                cg.inc(G.dma_start(SW[i % 2], s1w_h[i, :, :]
                                   .rearrange("(k p) f -> p k f", p=128)), 16)
                t_wl = cg.n
                for half in range(2):
                    nb = (i * 2 + half) % 2
                    pb = P[nb]
                    _w(T, (cg, t_wl), (cs, ffs.get(nb, 0)))
                    for k in range(ND):
                        inst = T.matmul(pb[:],
                                        SW[i % 2][:, k * 256 + half * 128:
                                                  k * 256 + (half + 1) * 128],
                                        XNT[:, k * T_CORE + tc * 512:
                                            k * T_CORE + (tc + 1) * 512],
                                        start=(k == 0), stop=(k == ND - 1))
                    ct.inc(inst)
                    _w(S, (ct, ct.n), (cy, t_sb))
                    ffs[nb] = cs.inc(S.activation(
                        HP[:, half * 512:(half + 1) * 512], pb[:],
                        AF.Identity, bias=S1B[:, 16 * half + i:16 * half + i + 1]))
                _w(S, (cs, cs.n))
                cs.inc(S.activation(GL[:], HP[:, 512:1024], AF.Gelu))
                _w(V, (cs, cs.n), (ct, t_abw))
                cv.inc(V.scalar_tensor_tensor(SAB[:, i * 512:(i + 1) * 512],
                                              GL[:], SGG[:, i:i + 1],
                                              HP[:, 0:512], ALU.mult, ALU.mult))
            t_a = cv.n
            _w(T, (cv, t_a), (cg, t_sw))
            for ts4 in range(4):
                for dh in range(2):
                    nb = 2 + (ts4 * 2 + dh) % 2
                    pb = P[nb]
                    _w(T, (cv, ffv.get(nb, 0)))
                    for ft in range(16):
                        inst = T.matmul(pb[:],
                                        SAB[:, ft * 512 + ts4 * 128:
                                           ft * 512 + (ts4 + 1) * 128],
                                        SF2[:, ft * D + dh * 512:
                                            ft * D + (dh + 1) * 512],
                                        start=(ft == 0), stop=(ft == 15))
                    ct.inc(inst)
                    _w(V, (ct, ct.n))
                    i = tc * 4 + ts4
                    sl = ACC[:, i * D + dh * 512:i * D + (dh + 1) * 512]
                    ffv[nb] = cv.inc(V.tensor_tensor(sl, sl, pb[:], ALU.add))
            sfc2_done[tc] = ct.n
        t_final = cv.n

        _w(G, (cv, t_final))
        for i in range(4):
            co.inc(G.dma_start(out_h[i * 512:(i + 1) * 512, :]
                               .rearrange("(n p) d -> p n d", p=128),
                               ACC[:, i * 4 * D:(i + 1) * 4 * D]), 16)
        G.wait_ge(co.h, co.n)
    return nc



def _np(a):
    return np.ascontiguousarray(np.asarray(a), dtype=np.float32)


def kernel(**kw):
    x = _np(kw["L0_attn_query"])
    W_qkv0 = _np(kw["L0_attn_W_qkv"])
    b_qkv0 = _np(kw["L0_attn_b_qkv"])
    W_out0 = _np(kw["L0_attn_W_out"])
    b_out0 = _np(kw["L0_attn_b_out"])
    wq_w = _np(kw["L0_attn_weight_q"])
    wv_w = _np(kw["L0_attn_weight_v"])
    gate0 = _np(kw["L0_ffn_gate_w"])
    bias_e = _np(kw["L0_ffn_bias_e"])
    up_w = _np(kw["L0_ffn_up_w"])
    down_w = _np(kw["L0_ffn_down_w"])
    sh_up = _np(kw["L0_ffn_shared_up"])
    sh_dn = _np(kw["L0_ffn_shared_down"])
    W_qkv1 = _np(kw["L1_attn_W_qkv"])
    b_qkv1 = _np(kw["L1_attn_b_qkv"])
    gate1 = _np(kw["L1_ffn_gate_w"])
    fc1_w = _np(kw["L1_ffn_fc1_w"])
    fc1_b = _np(kw["L1_ffn_fc1_b"])
    geglu = _np(kw["L1_ffn_geglu_mult"])
    fc2_w = _np(kw["L1_ffn_fc2_w"])
    fc2_b = _np(kw["L1_ffn_fc2_b"])
    s_fc1_w = _np(kw["L1_ffn_s_fc1_w"])
    s_fc1_b = _np(kw["L1_ffn_s_fc1_b"])
    s_geglu = _np(kw["L1_ffn_s_geglu_mult"])
    s_fc2_w = _np(kw["L1_ffn_s_fc2_w"])
    s_fc2_b = _np(kw["L1_ffn_s_fc2_b"])

    C = np.ascontiguousarray

    def pcol(v):
        k = v.size // 128
        return C(v.reshape(k, 128).T)

    shared_A = {
        "wq0T": C(W_qkv0[:1024].T), "wv0T": C(W_qkv0[2048:].T),
        "bq0": pcol(b_qkv0[:1024]), "bv0": pcol(b_qkv0[2048:]),
        "wavq": C(wq_w.transpose(0, 2, 1).reshape(1024, 3)
                  .reshape(8, 128, 3).transpose(1, 0, 2).reshape(128, 24)),
        "wavv": C(wv_w.transpose(0, 2, 1).reshape(1024, 3)
                  .reshape(8, 128, 3).transpose(1, 0, 2).reshape(128, 24)),
        "woT": C(W_out0.T), "bo": C(b_out0[None, :]),
        "gateT": C(gate0.T), "biase": C(np.tile(bias_e[None, :], (128, 1))),
        "upT": C(up_w.transpose(0, 2, 1)), "dnT": C(down_w.transpose(0, 2, 1)),
        "supT": C(sh_up.T), "sdnT": C(sh_dn.T),
        "w1T": C(W_qkv1.T),
        "b1qk": C(np.concatenate([pcol(b_qkv1[:1024]), pcol(b_qkv1[1024:2048])], 1)),
        "b1v": C(b_qkv1[2048:][None, :]),
        "idb": np.eye(128, dtype=np.float32),
        "onesr": np.ones((1, 128), np.float32),
        "ones4": np.ones((128, 4), np.float32),
        "wsel": C(np.tile((2.0 ** (15 - np.arange(16, dtype=np.float32)))[None, :],
                          (128, 1))),
    }
    ii = np.arange(128)[:, None]
    jj = np.arange(512)[None, :]
    masks = np.concatenate(
        [np.where(ii <= jj - 128 * m, 0.0, -BIG).astype(np.float32) for m in range(4)],
        axis=1)
    sf1T = C(s_fc1_w.T)
    s1P = C(np.concatenate([sf1T[:, :2048].reshape(1024, 16, 128),
                            sf1T[:, 2048:].reshape(1024, 16, 128)], axis=2)
            .transpose(1, 0, 2))
    shared_B = {
        "g1T": C(gate1.T),
        "f1T": C(fc1_w.transpose(0, 2, 1)),
        "f1b": C(fc1_b.reshape(15, 16, 128).transpose(0, 2, 1)),
        "geg": C(geglu.reshape(15, 8, 128).transpose(0, 2, 1)),
        "f2T": C(fc2_w.transpose(0, 2, 1)),
        "fbaug": C(np.concatenate([fc2_b, s_fc2_b[None, :]], 0)),
        "s1P": s1P,
        "s1b": C(s_fc1_b.reshape(32, 128).T),
        "sgeg": C(s_geglu.reshape(16, 128).T),
        "s2T": C(s_fc2_w.T),
        "masks": masks,
        "idb": np.eye(128, dtype=np.float32),
        "onesr": np.ones((1, 128), np.float32),
        "ones4": np.ones((128, 4), np.float32),
    }

    ncA = build_A()
    in_maps = []
    for core in range(8):
        b, j = core // 4, core % 4
        xp = np.concatenate([x[b, j * 1024:(j + 1) * 1024],
                             x[b, (7 - j) * 1024:(8 - j) * 1024]], axis=0)
        m = dict(shared_A)
        m["x"] = C(xp)
        in_maps.append(m)
    resA = run_bass_kernel_spmd(ncA, in_maps, core_ids=list(range(8)))
    RA = resA.results

    import ml_dtypes
    ktF = np.zeros((2, D, 8192), dtype=ml_dtypes.bfloat16)
    vF = np.zeros((2, 8192, D), dtype=ml_dtypes.bfloat16)
    for core in range(8):
        b, j = core // 4, core % 4
        kt = RA[core]["kT"]
        vv = RA[core]["vN"]
        ktF[b][:, j * 1024:(j + 1) * 1024] = kt[:, :1024]
        ktF[b][:, (7 - j) * 1024:(8 - j) * 1024] = kt[:, 1024:]
        vF[b][j * 1024:(j + 1) * 1024] = vv[:1024]
        vF[b][(7 - j) * 1024:(8 - j) * 1024] = vv[1024:]

    ncB = build_B()
    out = np.empty((2, 8192, D), dtype=np.float32)
    mapsB = []
    NKB_TRUE = lambda j, g: (8 * j + 4, 8 * j + 8, 8 * (7 - j) + 4,
                             8 * (7 - j) + 8)[g]
    for core in range(8):
        b, j = core // 4, core % 4
        ktG = np.zeros((D, 192 * 128), dtype=ml_dtypes.bfloat16)
        vG = np.zeros((192 * 128, D), dtype=ml_dtypes.bfloat16)
        vmask = np.zeros((128, 192), dtype=np.float32)
        for g in range(4):
            nfix = NKB_FIX[g]
            ntrue = NKB_TRUE(j, g)
            pad = nfix - ntrue
            c0 = SEG_OFF[g] * 128
            ktG[:, c0 + pad * 128:c0 + nfix * 128] = ktF[b][:, :ntrue * 128]
            vG[c0 + pad * 128:c0 + nfix * 128, :] = vF[b][:ntrue * 128, :]
            vmask[:, SEG_OFF[g] + pad:SEG_OFF[g] + nfix] = 1.0
        m = dict(shared_B)
        m["out1"] = RA[core]["out1"]
        m["xn1T"] = RA[core]["xn1T"]
        m["qsT"] = RA[core]["qsT"]
        m["ktG"] = ktG
        m["vG"] = vG
        m["vmask"] = vmask
        mapsB.append(m)
    resB = run_bass_kernel_spmd(ncB, mapsB, core_ids=list(range(8)))
    for core in range(8):
        b, j = core // 4, core % 4
        o = resB.results[core]["out"]
        out[b, j * 1024:(j + 1) * 1024] = o[:1024]
        out[b, (7 - j) * 1024:(8 - j) * 1024] = o[1024:]
    return out


# version 10
# speedup vs baseline: 3.1727x; 1.0944x over previous
import math
import numpy as np
from contextlib import ExitStack

import concourse.bass as bass
import concourse.mybir as mybir
from concourse.bass_utils import run_bass_kernel_spmd

F32 = mybir.dt.float32
BF16 = mybir.dt.bfloat16
AF = mybir.ActivationFunctionType
ALU = mybir.AluOpType
AX = mybir.AxisListType

T_CORE = 2048
NTT = 16
ND = 8
D = 1024
BIG = 1.0e30
ISQ2 = 0.7071067811865476


class Ch:
    def __init__(self, nc, st, name):
        self.h = st.enter_context(nc.semaphore(name))
        self.n = 0

    def inc(self, instr, k=1):
        instr.then_inc(self.h, k)
        self.n += k
        return self.n


def _w(eng, *tok):
    for ch, v in tok:
        if v > 0:
            eng.wait_ge(ch.h, v)



def build_A():
    nc = bass.Bass("TRN2", target_bir_lowering=False)
    dt = nc.dram_tensor
    x_h = dt("x", [T_CORE, D], F32, kind="ExternalInput")
    wq0_h = dt("wq0T", [D, D], F32, kind="ExternalInput")
    wv0_h = dt("wv0T", [D, D], F32, kind="ExternalInput")
    bq0_h = dt("bq0", [128, 8], F32, kind="ExternalInput")
    bv0_h = dt("bv0", [128, 8], F32, kind="ExternalInput")
    wvq_h = dt("wavq", [128, 24], F32, kind="ExternalInput")
    wvv_h = dt("wavv", [128, 24], F32, kind="ExternalInput")
    wo_h = dt("woT", [D, D], F32, kind="ExternalInput")
    bo_h = dt("bo", [1, D], F32, kind="ExternalInput")
    gw_h = dt("gateT", [D, 16], F32, kind="ExternalInput")
    be_h = dt("biase", [128, 16], F32, kind="ExternalInput")
    up_h = dt("upT", [16, D, 512], F32, kind="ExternalInput")
    dn_h = dt("dnT", [16, 512, D], F32, kind="ExternalInput")
    sup_h = dt("supT", [D, 2048], F32, kind="ExternalInput")
    sdn_h = dt("sdnT", [2048, D], F32, kind="ExternalInput")
    w1_h = dt("w1T", [D, 3072], F32, kind="ExternalInput")
    b1qk_h = dt("b1qk", [128, 16], F32, kind="ExternalInput")
    b1v_h = dt("b1v", [1, D], F32, kind="ExternalInput")
    idb_h = dt("idb", [128, 128], F32, kind="ExternalInput")
    onesr_h = dt("onesr", [1, 128], F32, kind="ExternalInput")
    ones4_h = dt("ones4", [128, 4], F32, kind="ExternalInput")
    wsel_h = dt("wsel", [128, 16], F32, kind="ExternalInput")
    out1_h = dt("out1", [T_CORE, D], F32, kind="ExternalOutput")
    xn1_h = dt("xn1T", [D, T_CORE], BF16, kind="ExternalOutput")
    qs_h = dt("qsT", [D, T_CORE], BF16, kind="ExternalOutput")
    kt_h = dt("kT", [D, T_CORE], BF16, kind="ExternalOutput")
    vo_h = dt("vN", [T_CORE, D], BF16, kind="ExternalOutput")

    st = ExitStack()
    with st:
        sb = lambda n, s, d_: st.enter_context(nc.sbuf_tensor(n, s, d_))
        pst = lambda n, s, d_: st.enter_context(nc.psum_tensor(n, s, d_))
        ACC = sb("ACC", [128, NTT * D], F32)
        XNT = sb("XNT", [128, ND * T_CORE], BF16)
        AT = sb("AT", [128, ND * T_CORE], BF16)
        HF = sb("HF", [128, 6144], F32)
        SCR = sb("SCR", [128, 4096], F32)
        WU = [sb(f"WU{i}", [128, 8 * 512], BF16) for i in range(2)]
        WBIG = sb("WBIG", [128, 8 * D], BF16)
        WD = [WBIG[:, 0:4096], WBIG[:, 4096:8192]]
        WQS = [WU[0][:, 0:1024], WU[1][:, 0:1024]]
        WVS = [WU[0][:, 1024:2048], WU[1][:, 1024:2048]]
        WO = WBIG[:, 0:8192]
        W1V = WBIG[:, 0:8192]
        XNB0 = sb("XNB0", [128, D], BF16)
        XNB = [XNB0, XNB0]
        STG = [AT[:, 0:2048], AT[:, 2048:4096]]
        HBOF = 4096
        BQ0 = sb("BQ0", [128, 8], F32)
        BV0 = sb("BV0", [128, 8], F32)
        WAQ = sb("WAQ", [128, 24], F32)
        WAV = sb("WAV", [128, 24], F32)
        WQE = sb("WQE", [128, 24], F32)
        BO = SCR[0:1, 1024:2048]
        B1V = BO
        GW = sb("GW", [128, 8 * 16], BF16)
        BE = sb("BE", [128, 16], F32)
        B1QK = sb("B1QK", [128, 16], F32)
        B1Q32 = sb("B1Q32", [128, 8], F32)
        EPSB = sb("EPSB", [128, 1], F32)
        IDB = sb("IDB", [128, 128], BF16)
        ONR = sb("ONR", [1, 128], F32)
        ON4 = sb("ON4", [128, 4], F32)
        WSEL = sb("WSEL", [128, 16], F32)
        SCALL = sb("SCALL", [128, NTT * 16], F32)
        SC2 = sb("SC2", [128, 16], F32)
        GS = sb("GS", [128, 4], F32)
        GM = sb("GM", [128, 4], F32)
        SMASK = sb("SMASK", [128, 16], F32)
        SEL = sb("SEL", [128, 16], F32)
        RSC = sb("RSC", [128, 8], F32)
        COMB = sb("COMB", [128, NTT * 16], F32)
        SQT = SCR[:, 0:512]
        P = [pst(f"P{i}", [128, 512], F32) for i in range(4)]
        PT = [pst(f"PTp{i}", [128, 128], BF16) for i in range(2)]

        cg = Ch(nc, st, "cg")
        cy = Ch(nc, st, "cy")
        ct = Ch(nc, st, "ct")
        cv = Ch(nc, st, "cv")
        cs = Ch(nc, st, "cs")
        co = Ch(nc, st, "co")
        G, Y, T, V, S = nc.gpsimd, nc.sync, nc.tensor, nc.vector, nc.scalar
        pfv = {0: 0, 1: 0, 2: 0, 3: 0}
        pfs = {0: 0, 1: 0, 2: 0, 3: 0}

        for i in range(4):
            cy.inc(Y.dma_start(ACC[:, i * 4 * D:(i + 1) * 4 * D],
                               x_h[i * 512:(i + 1) * 512, :]
                               .rearrange("(n p) d -> p n d", p=128)), 16)
        for dst, srch in ((BQ0, bq0_h), (BV0, bv0_h), (WAQ, wvq_h), (WAV, wvv_h),
                          (BE, be_h), (B1QK, b1qk_h),
                          (ONR, onesr_h), (ON4, ones4_h), (WSEL, wsel_h)):
            cy.inc(Y.dma_start(dst[:], srch[:, :]), 16)
        t_loads = cy.n
        cg.inc(G.dma_start(IDB[:], idb_h[:, :]), 16)
        cg.inc(G.dma_start(GW[:], gw_h[:, :].rearrange("(k p) e -> p k e", p=128)), 16)
        t_gw = cg.n

        _w(V, (cy, t_loads))
        cv.inc(V.tensor_scalar_mul(WQE[:, 0:24:3], WAQ[:, 0:24:3], ISQ2 ** 9))
        cv.inc(V.tensor_scalar_mul(WQE[:, 1:24:3], WAQ[:, 1:24:3], ISQ2 ** 9))
        cv.inc(V.tensor_scalar_mul(WQE[:, 2:24:3], WAQ[:, 2:24:3], ISQ2 ** 6))
        cv.inc(V.tensor_scalar_mul(B1Q32[:], B1QK[:, 0:8], 1.0 / 32.0))
        cv.inc(V.memset(EPSB[:], 1e-5))
        t_prep = cv.n

        def layernorm_and_transpose(dep_v, dep_s):
            _w(S, (cv, dep_v), (cs, dep_s))
            _w(V, (cv, dep_v), (cs, dep_s))
            _w(T, (cg, 16))
            trans_tok = {}
            for i in range(NTT):
                a = ACC[:, i * D:(i + 1) * D]
                _w(S, (cv, cv.n))
                cs.inc(S.activation(SQT[:], a[:, 0:512], AF.Square))
                _w(V, (cs, cs.n))
                cv.inc(V.reduce_sum(RSC[:, 1:2], SQT[:], axis=AX.X))
                _w(S, (cv, cv.n))
                cs.inc(S.activation(SQT[:], a[:, 512:1024], AF.Square))
                _w(V, (cs, cs.n))
                cv.inc(V.reduce_sum(RSC[:, 2:3], SQT[:], axis=AX.X))
                cv.inc(V.reduce_sum(RSC[:, 0:1], a, axis=AX.X))
                _w(V, (cv, cv.n))
                cv.inc(V.tensor_add(RSC[:, 1:2], RSC[:, 1:2], RSC[:, 2:3]))
                _w(V, (cv, cv.n))
                cv.inc(V.tensor_scalar_mul(RSC[:, 0:1], RSC[:, 0:1], 1.0 / D))
                _w(V, (cv, cv.n))
                cv.inc(V.tensor_scalar_mul(RSC[:, 1:2], RSC[:, 1:2], 1.0 / D))
                cv.inc(V.scalar_tensor_tensor(RSC[:, 3:4], RSC[:, 0:1], -1.0,
                                              RSC[:, 0:1], ALU.mult, ALU.mult))
                _w(V, (cv, cv.n))
                cv.inc(V.tensor_add(RSC[:, 1:2], RSC[:, 1:2], RSC[:, 3:4]))
                _w(S, (cv, cv.n))
                cs.inc(S.activation(RSC[:, 4:5], RSC[:, 1:2], AF.Sqrt,
                                    bias=EPSB[:, 0:1]))
                _w(V, (cs, cs.n))
                cv.inc(V.reciprocal(RSC[:, 5:6], RSC[:, 4:5]))
                _w(V, (cv, cv.n))
                xb = XNB[i % 2]
                if i >= 1:
                    _w(V, (ct, trans_tok[i - 1]))
                t_x = cv.inc(V.tensor_scalar(xb[:], a, RSC[:, 0:1], RSC[:, 5:6],
                                             ALU.subtract, ALU.mult))
                ptfree = getattr(layernorm_and_transpose, "_ptfree", {0: 0, 1: 0})
                for k in range(ND):
                    _w(T, (cv, t_x), (cv, ptfree[k % 2]))
                    ct.inc(T.transpose(PT[k % 2][:], xb[:, k * 128:(k + 1) * 128], IDB[:]))
                    _w(V, (ct, ct.n))
                    ptfree[k % 2] = cv.inc(V.tensor_copy(
                        XNT[:, k * T_CORE + i * 128:k * T_CORE + (i + 1) * 128],
                        PT[k % 2][:]))
                layernorm_and_transpose._ptfree = ptfree
                trans_tok[i] = ct.n
            return cv.n

        t_ln0 = layernorm_and_transpose(t_prep, 0)

        qv_done = {}
        for e in range(ND):
            if e >= 2:
                _w(G, (ct, qv_done[e - 2]))
            else:
                _w(G, (cv, t_ln0))
            cg.inc(G.dma_start(WQS[e % 2],
                               wq0_h[:, e * 128:(e + 1) * 128]
                               .rearrange("(k p) c -> p k c", p=128)), 16)
            cg.inc(G.dma_start(WVS[e % 2],
                               wv0_h[:, e * 128:(e + 1) * 128]
                               .rearrange("(k p) c -> p k c", p=128)), 16)
            t_wl = cg.n
            _w(T, (cg, t_wl), (cv, t_ln0))

            def proj(wbuf, bias, banks):
                for tb in range(4):
                    nb = banks[tb % 2]
                    pb = P[nb]
                    _w(T, (cv, pfv[nb]), (cs, pfs[nb]))
                    for k in range(ND):
                        inst = T.matmul(pb[:], wbuf[:, k * 128:(k + 1) * 128],
                                        XNT[:, k * T_CORE + tb * 512:
                                            k * T_CORE + (tb + 1) * 512],
                                        start=(k == 0), stop=(k == ND - 1))
                    ct.inc(inst)
                    _w(V, (ct, ct.n))
                    pfv[nb] = cv.inc(V.tensor_scalar_add(
                        HF[:, 2048 + tb * 512:2048 + (tb + 1) * 512], pb[:],
                        bias[:, e:e + 1]))

            def wavedec(co):
                _w(V, (cv, cv.n))
                e0, o0 = HF[:, 2048:4096:2], HF[:, 2049:4096:2]
                cv.inc(V.tensor_sub(HF[:, co:co + 1024], e0, o0))
                cv.inc(V.tensor_add(SCR[:, 0:1024], e0, o0))
                _w(V, (cv, cv.n))
                e1, o1 = SCR[:, 0:1024:2], SCR[:, 1:1024:2]
                cv.inc(V.tensor_sub(HF[:, co + 1024:co + 1536], e1, o1))
                cv.inc(V.tensor_add(SCR[:, 1024:1536], e1, o1))
                _w(V, (cv, cv.n))
                e2, o2 = SCR[:, 1024:1536:2], SCR[:, 1025:1536:2]
                cv.inc(V.tensor_sub(HF[:, co + 1536:co + 1792], e2, o2))
                cv.inc(V.tensor_add(HF[:, co + 1792:co + 2048], e2, o2))

            proj(WQS[e % 2], BQ0, (0, 1))
            wavedec(0)
            proj(WVS[e % 2], BV0, (2, 3))
            wavedec(4096)
            cq, cvo = 0, 4096
            _w(V, (cv, cv.n))
            for (qo, vo, j, ln, po) in ((cq + 1792, cvo + 1792, 0, 256, 2048),
                                        (cq + 1536, cvo + 1536, 1, 256, 2304),
                                        (cq + 1024, cvo + 1024, 2, 512, 2560)):
                cv.inc(V.tensor_scalar_mul(SCR[:, 0:ln], HF[:, qo:qo + ln],
                                           WQE[:, e * 3 + j:e * 3 + j + 1]))
                cv.inc(V.tensor_scalar_mul(SCR[:, 1024:1024 + ln], HF[:, vo:vo + ln],
                                           WAV[:, e * 3 + j:e * 3 + j + 1]))
                _w(V, (cv, cv.n))
                cv.inc(V.tensor_mul(SCR[:, po:po + ln], SCR[:, 0:ln],
                                    SCR[:, 1024:1024 + ln]))
                _w(V, (cv, cv.n))
            cv.inc(V.tensor_scalar_mul(SCR[:, 0:1024], HF[:, cq:cq + 1024], ISQ2 ** 3))
            _w(V, (cv, cv.n))
            cv.inc(V.tensor_mul(SCR[:, 3072:4096], SCR[:, 0:1024],
                                HF[:, cvo:cvo + 1024]))
            _w(V, (cv, cv.n))
            cv.inc(V.tensor_add(SCR[:, 0:512:2], SCR[:, 2048:2304], SCR[:, 2304:2560]))
            cv.inc(V.tensor_sub(SCR[:, 1:512:2], SCR[:, 2048:2304], SCR[:, 2304:2560]))
            _w(V, (cv, cv.n))
            cv.inc(V.tensor_add(SCR[:, 512:1536:2], SCR[:, 0:512], SCR[:, 2560:3072]))
            cv.inc(V.tensor_sub(SCR[:, 513:1536:2], SCR[:, 0:512], SCR[:, 2560:3072]))
            _w(V, (cv, cv.n))
            at = AT[:, e * T_CORE:(e + 1) * T_CORE]
            cv.inc(V.tensor_add(at[:, 0:T_CORE:2], SCR[:, 512:1536], SCR[:, 3072:4096]))
            cv.inc(V.tensor_sub(at[:, 1:T_CORE:2], SCR[:, 512:1536], SCR[:, 3072:4096]))
            qv_done[e] = ct.n
        t_at = cv.n

        cg.inc(G.dma_start(WO[:], wo_h[:, :].rearrange("(k p) e -> p k e", p=128)), 16)
        t_wo = cg.n
        _w(Y, (cv, t_at), (cs, cs.n))
        cy.inc(Y.dma_start(BO, bo_h[:, :]), 16)
        t_bo = cy.n

        _w(T, (cg, t_wo), (cv, t_at), (cy, t_bo))
        for i in range(NTT):
            for dh in range(2):
                nb = (2 * i + dh) % 2
                pb = P[nb]
                _w(T, (cv, pfv[nb]), (cs, pfs[nb]))
                for k in range(ND):
                    T.matmul(pb[:], AT[:, k * T_CORE + i * 128:k * T_CORE + (i + 1) * 128],
                             WO[:, k * D + dh * 512:k * D + (dh + 1) * 512],
                             start=(k == 0), stop=False)
                ct.inc(T.matmul(pb[:], ONR[:, :], BO[0:1, dh * 512:(dh + 1) * 512],
                                start=False, stop=True))
                _w(V, (ct, ct.n))
                sl = ACC[:, i * D + dh * 512:i * D + (dh + 1) * 512]
                pfv[nb] = cv.inc(V.tensor_tensor(sl, sl, pb[:], ALU.add))
        t_attn = cv.n

        _w(T, (cg, t_gw))
        scall_tok = {}
        for i in range(NTT):
            _w(T, (cs, pfs[2]))
            for k in range(ND):
                inst = T.matmul(P[2][:, 0:16],
                                XNT[:, k * T_CORE + i * 128:k * T_CORE + (i + 1) * 128],
                                GW[:, k * 16:(k + 1) * 16],
                                start=(k == 0), stop=(k == ND - 1))
            ct.inc(inst)
            _w(S, (ct, ct.n))
            pfs[2] = cs.inc(S.activation(SCALL[:, i * 16:(i + 1) * 16],
                                         P[2][:, 0:16], AF.Sigmoid))
            scall_tok[i] = cs.n
        comb_tok = {}
        for i in range(NTT):
            SCORES = SCALL[:, i * 16:(i + 1) * 16]
            _w(V, (cs, scall_tok[i]), (cv, cv.n))
            cv.inc(V.tensor_add(SC2[:], SCORES[:], BE[:]))
            for g in range(4):
                _w(V, (cv, cv.n))
                sl = SC2[:, 4 * g:4 * g + 4]
                cv.inc(V.reduce_max(RSC[:, 0:1], sl, axis=AX.X))
                _w(V, (cv, cv.n))
                cv.inc(V.tensor_scalar(SMASK[:, 0:4], sl, RSC[:, 0:1], None,
                                       ALU.is_equal))
                _w(V, (cv, cv.n))
                cv.inc(V.scalar_tensor_tensor(SMASK[:, 4:8], SMASK[:, 0:4], -BIG, sl,
                                              ALU.mult, ALU.add))
                _w(V, (cv, cv.n))
                cv.inc(V.reduce_max(RSC[:, 1:2], SMASK[:, 4:8], axis=AX.X))
                _w(V, (cv, cv.n))
                cv.inc(V.tensor_add(GS[:, g:g + 1], RSC[:, 0:1], RSC[:, 1:2]))
            _w(V, (cv, cv.n))
            cv.inc(V.reduce_max(RSC[:, 0:1], GS[:], axis=AX.X))
            _w(V, (cv, cv.n))
            cv.inc(V.tensor_scalar(GM[:], GS[:], RSC[:, 0:1], None, ALU.is_equal))
            _w(V, (cv, cv.n))
            cv.inc(V.scalar_tensor_tensor(GM[:], GM[:], -BIG, GS[:], ALU.mult, ALU.add))
            _w(V, (cv, cv.n))
            cv.inc(V.reduce_max(RSC[:, 1:2], GM[:], axis=AX.X))
            _w(V, (cv, cv.n))
            cv.inc(V.tensor_scalar(GM[:], GS[:], RSC[:, 1:2], None, ALU.is_ge))
            _w(V, (cv, cv.n))
            for g in range(4):
                cv.inc(V.tensor_scalar_mul(SMASK[:, 4 * g:4 * g + 4], ON4[:],
                                           GM[:, g:g + 1]))
            _w(V, (cv, cv.n))
            cv.inc(V.tensor_mul(SC2[:], SC2[:], SMASK[:]))
            cv.inc(V.memset(SEL[:], 0.0))
            for it in range(4):
                _w(V, (cv, cv.n))
                cv.inc(V.reduce_max(RSC[:, 0:1], SC2[:], axis=AX.X))
                _w(V, (cv, cv.n))
                cv.inc(V.tensor_scalar(SMASK[:], SC2[:], RSC[:, 0:1], None,
                                       ALU.is_equal))
                _w(V, (cv, cv.n))
                cv.inc(V.tensor_mul(SMASK[:], SMASK[:], WSEL[:]))
                _w(V, (cv, cv.n))
                cv.inc(V.reduce_max(RSC[:, 1:2], SMASK[:], axis=AX.X))
                _w(V, (cv, cv.n))
                cv.inc(V.tensor_scalar(SMASK[:], SMASK[:], RSC[:, 1:2], None,
                                       ALU.is_equal))
                _w(V, (cv, cv.n))
                cv.inc(V.tensor_add(SEL[:], SEL[:], SMASK[:]))
                cv.inc(V.scalar_tensor_tensor(SC2[:], SMASK[:], -BIG, SC2[:],
                                              ALU.mult, ALU.add))
            _w(V, (cv, cv.n))
            cv.inc(V.tensor_mul(SEL[:], SEL[:], SCORES[:]))
            _w(V, (cv, cv.n))
            cv.inc(V.reduce_sum(RSC[:, 0:1], SEL[:], axis=AX.X))
            _w(V, (cv, cv.n))
            cv.inc(V.tensor_scalar_add(RSC[:, 0:1], RSC[:, 0:1], 1e-20))
            _w(V, (cv, cv.n))
            cv.inc(V.reciprocal(RSC[:, 1:2], RSC[:, 0:1]))
            _w(V, (cv, cv.n))
            comb_tok[i] = cv.inc(V.tensor_scalar(COMB[:, i * 16:(i + 1) * 16],
                                                 SEL[:], RSC[:, 1:2],
                                                 2.5, ALU.mult, ALU.mult))
        t_comb = cv.n

        moe_done = {}
        for e in range(20):
            shared = e >= 16
            c = e - 16
            if e >= 2:
                _w(G, (ct, moe_done[e - 2]))
            else:
                _w(G, (ct, ct.n))
            if shared:
                cg.inc(G.dma_start(WU[e % 2][:],
                                   sup_h[:, c * 512:(c + 1) * 512]
                                   .rearrange("(k p) f -> p k f", p=128)), 16)
                cg.inc(G.dma_start(WD[e % 2],
                                   sdn_h[c * 512:(c + 1) * 512, :]
                                   .rearrange("(k p) d -> p k d", p=128)), 16)
            else:
                cg.inc(G.dma_start(WU[e % 2][:],
                                   up_h[e, :, :].rearrange("(k p) f -> p k f", p=128)), 16)
                cg.inc(G.dma_start(WD[e % 2],
                                   dn_h[e, :, :].rearrange("(k p) d -> p k d", p=128)), 16)
            t_wl = cg.n
            _w(T, (cg, t_wl), (cv, t_attn))
            for ft in range(4):
                for tb in range(4):
                    nb = (ft * 4 + tb) % 2
                    pb = P[nb]
                    _w(T, (cs, pfs[nb]), (cv, pfv[nb]))
                    for k in range(ND):
                        inst = T.matmul(pb[:],
                                        WU[e % 2][:, k * 512 + ft * 128:
                                                  k * 512 + (ft + 1) * 128],
                                        XNT[:, k * T_CORE + tb * 512:
                                            k * T_CORE + (tb + 1) * 512],
                                        start=(k == 0), stop=(k == ND - 1))
                    ct.inc(inst)
                    _w(S, (ct, ct.n))
                    if ft < 3:
                        sl = HF[:, ft * T_CORE + tb * 512:ft * T_CORE + (tb + 1) * 512]
                    else:
                        sl = SCR[:, tb * 512:(tb + 1) * 512]
                    cs.inc(S.activation(sl, pb[:], AF.Relu))
                    _w(S, (cs, cs.n))
                    pfs[nb] = cs.inc(
                        S.activation(AT[:, HBOF + ft * T_CORE + tb * 512:
                                        HBOF + ft * T_CORE + (tb + 1) * 512],
                                     sl, AF.Square))
            t_h = cs.n
            _w(T, (cs, t_h))
            for i in range(NTT):
                for dh in range(2):
                    nb = 2 + (2 * i + dh) % 2
                    pb = P[nb]
                    _w(T, (cv, pfv[nb]), (cs, pfs[nb]))
                    for ft in range(4):
                        inst = T.matmul(pb[:],
                                        AT[:, HBOF + ft * T_CORE + i * 128:
                                           HBOF + ft * T_CORE + (i + 1) * 128],
                                        WD[e % 2][:, ft * D + dh * 512:
                                                  ft * D + (dh + 1) * 512],
                                        start=(ft == 0), stop=(ft == 3))
                    ct.inc(inst)
                    _w(V, (ct, ct.n))
                    if not shared:
                        _w(V, (cv, comb_tok[i]))
                    sl = ACC[:, i * D + dh * 512:i * D + (dh + 1) * 512]
                    if shared:
                        pfv[nb] = cv.inc(V.tensor_tensor(sl, sl, pb[:], ALU.add))
                    else:
                        pfv[nb] = cv.inc(V.scalar_tensor_tensor(
                            sl, pb[:], COMB[:, i * 16 + e:i * 16 + e + 1], sl,
                            ALU.mult, ALU.add))
            moe_done[e] = ct.n
        t_moe = cv.n

        _w(G, (cv, t_moe))
        for i in range(4):
            co.inc(G.dma_start(out1_h[i * 512:(i + 1) * 512, :]
                               .rearrange("(n p) d -> p n d", p=128),
                               ACC[:, i * 4 * D:(i + 1) * 4 * D]), 16)
        t_ln1 = layernorm_and_transpose(t_moe, 0)
        _w(G, (cv, t_ln1))
        co.inc(G.dma_start(xn1_h[:, :].rearrange("(k p) t -> p k t", p=128),
                           XNT[:]), 16)

        _w(G, (ct, ct.n))
        cg.inc(G.dma_start(W1V,
                           w1_h[:, 2048:3072].rearrange("(k p) e -> p k e", p=128)), 16)
        _w(Y, (ct, ct.n), (cs, cs.n))
        cy.inc(Y.dma_start(B1V[:], b1v_h[:, :]), 16)
        t_w1v = cg.n
        for which in range(2):
            cofs = 0 if which == 0 else 1024
            out_hbm = qs_h if which == 0 else kt_h
            for e in range(ND):
                _w(G, (ct, ct.n))
                cg.inc(G.dma_start(WQS[e % 2],
                                   w1_h[:, cofs + e * 128:cofs + (e + 1) * 128]
                                   .rearrange("(k p) c -> p k c", p=128)), 16)
                _w(T, (cg, cg.n), (cv, t_ln1), (co, co.n))
                stg = STG[e % 2]
                for tb in range(4):
                    nb = tb % 2
                    pb = P[nb]
                    _w(T, (cs, pfs[nb]), (cv, pfv[nb]))
                    for k in range(ND):
                        inst = T.matmul(pb[:], WQS[e % 2][:, k * 128:(k + 1) * 128],
                                        XNT[:, k * T_CORE + tb * 512:
                                            k * T_CORE + (tb + 1) * 512],
                                        start=(k == 0), stop=(k == ND - 1))
                    ct.inc(inst)
                    _w(S, (ct, ct.n))
                    if which == 0:
                        pfs[nb] = cs.inc(S.activation(stg[:, tb * 512:(tb + 1) * 512],
                                            pb[:], AF.Identity, bias=B1Q32[:, e:e + 1],
                                            scale=1.0 / 32.0))
                    else:
                        pfs[nb] = cs.inc(S.activation(stg[:, tb * 512:(tb + 1) * 512],
                                            pb[:], AF.Identity,
                                            bias=B1QK[:, 8 + e:8 + e + 1]))
                _w(G, (cs, cs.n))
                co.inc(G.dma_start(out_hbm[e * 128:(e + 1) * 128, :], stg[:]), 16)
        _w(T, (cg, t_w1v))
        for i in range(NTT):
            stg = STG[i % 2]
            _w(T, (co, co.n))
            for eh in range(2):
                nb = (2 * i + eh) % 2
                pb = P[nb]
                _w(T, (cv, pfv[nb]), (cs, pfs[nb]))
                for k in range(ND):
                    T.matmul(pb[:], XNT[:, k * T_CORE + i * 128:k * T_CORE + (i + 1) * 128],
                             W1V[:, k * D + eh * 512:k * D + (eh + 1) * 512],
                             start=(k == 0), stop=False)
                ct.inc(T.matmul(pb[:], ONR[:, :], B1V[0:1, eh * 512:(eh + 1) * 512],
                                start=False, stop=True))
                _w(V, (ct, ct.n))
                pfv[nb] = cv.inc(V.tensor_copy(stg[:, eh * 512:(eh + 1) * 512], pb[:]))
            _w(G, (cv, cv.n))
            co.inc(G.dma_start(vo_h[i * 128:(i + 1) * 128, :], stg[:, 0:1024]), 16)
        G.wait_ge(co.h, co.n)
    return nc



NKB_FIX = [32, 32, 64, 64]
SEG_OFF = [0, 32, 64, 128]


def build_B(attn_only=False):
    nc = bass.Bass("TRN2", target_bir_lowering=False)
    dt = nc.dram_tensor
    if attn_only:
        dbg_negc = dt("dbg_negc", [1, 512], F32, kind="ExternalOutput")
        dbg_lr = dt("dbg_lr", [1, 4], F32, kind="ExternalOutput")
        dbg_yp = dt("dbg_yp", [128, 4096], F32, kind="ExternalOutput")
        dbg_pts = dt("dbg_pts", [128, 4096], BF16, kind="ExternalOutput")
    out1_h = dt("out1", [T_CORE, D], F32, kind="ExternalInput")
    xn1_h = dt("xn1T", [D, T_CORE], BF16, kind="ExternalInput")
    qs_h = dt("qsT", [D, T_CORE], BF16, kind="ExternalInput")
    ktf_h = dt("ktG", [D, 192 * 128], BF16, kind="ExternalInput")
    vf_h = dt("vG", [192 * 128, D], BF16, kind="ExternalInput")
    vmask_h = dt("vmask", [128, 192], F32, kind="ExternalInput")
    g1_h = dt("g1T", [D, 15], F32, kind="ExternalInput")
    f1w_h = dt("f1T", [15, D, 2048], F32, kind="ExternalInput")
    f1b_h = dt("f1b", [15, 128, 16], F32, kind="ExternalInput")
    geg_h = dt("geg", [15, 128, 8], F32, kind="ExternalInput")
    f2w_h = dt("f2T", [15, D, D], F32, kind="ExternalInput")
    fba_h = dt("fbaug", [16, D], F32, kind="ExternalInput")
    s1w_h = dt("s1P", [16, D, 256], F32, kind="ExternalInput")
    s1b_h = dt("s1b", [128, 32], F32, kind="ExternalInput")
    sgg_h = dt("sgeg", [128, 16], F32, kind="ExternalInput")
    s2w_h = dt("s2T", [2048, D], F32, kind="ExternalInput")
    msk_h = dt("masks", [128, 2048], F32, kind="ExternalInput")
    idb_h = dt("idb", [128, 128], F32, kind="ExternalInput")
    onesr_h = dt("onesr", [1, 128], F32, kind="ExternalInput")
    ones4_h = dt("ones4", [128, 4], F32, kind="ExternalInput")
    out_h = dt("out", [T_CORE, D], F32, kind="ExternalOutput")

    tile_base = [0, 4, 8, 12]

    st = ExitStack()
    with st:
        sb = lambda n, s, d_: st.enter_context(nc.sbuf_tensor(n, s, d_))
        pst = lambda n, s, d_: st.enter_context(nc.psum_tensor(n, s, d_))
        ACC = sb("ACC", [128, NTT * D], F32)
        XNT = sb("XNT", [128, ND * T_CORE], BF16)
        AR1 = sb("AR1", [128, 16384], BF16)
        AR2 = sb("AR2", [128, 16384], BF16)
        AR3 = sb("AR3", [128, 8192], BF16)
        AR4 = sb("AR4", [128, 7168], F32)
        IDF = sb("IDF", [128, 128], F32)
        ONR = sb("ONR", [1, 128], F32)
        ON4 = sb("ON4", [128, 4], F32)
        ONB = sb("ONB", [1, 128], BF16)
        VMB = sb("VMB", [128, 192], BF16)
        NEGCB = sb("NEGCB", [1, 512], BF16)
        LC = sb("LC", [128, 1], F32)
        RI = sb("RI", [128, 4], F32)
        RSC = sb("RSC", [128, 8], F32)
        ESC = sb("ESC", [128, 15], F32)
        SELB = sb("SELB", [128, 15], F32)
        TMP15 = sb("TMP15", [128, 15], F32)
        F1B = AR4[:, 5120:5136]
        GEG = AR4[:, 5136:5144]
        S1B = AR4[:, 5152:5184]
        SGG = AR4[:, 5184:5200]
        G1W = sb("G1W", [128, 8 * 15], BF16)
        P = [pst(f"P{i}", [128, 512], F32) for i in range(4)]
        PLL = pst("PLL", [1, 512], F32)
        PLC = pst("PLC", [128, 4], F32)
        PCT = pst("PCT", [16, 128], F32)

        KC = [AR1[:, 0:8192], AR1[:, 8192:16384]]
        VC = [AR2[:, 0:8192], AR2[:, 8192:16384]]
        QSG = AR3[:, 0:4096]
        PTS = AR3[:, 4096:8192]
        YP = AR4[:, 0:4096]
        MSK = AR4[:, 4096:6144]
        TMS = AR4[:, 6144:6656]
        SQ = AR4[:, 6144:6656]
        LR = AR4[0:1, 6656:7168]
        NQR = AR4[32:33, 6656:7168]
        NEGC = AR4[64:65, 6656:7168]
        MKR = AR4[0:1, 6144:6656]

        COMB1 = AR4[:, 1536:1792]
        cg = Ch(nc, st, "cg")
        cy = Ch(nc, st, "cy")
        ct = Ch(nc, st, "ct")
        cv = Ch(nc, st, "cv")
        cs = Ch(nc, st, "cs")
        co = Ch(nc, st, "co")
        G, Y, T, V, S = nc.gpsimd, nc.sync, nc.tensor, nc.vector, nc.scalar

        for i in range(4):
            cy.inc(Y.dma_start(ACC[:, i * 4 * D:(i + 1) * 4 * D],
                               out1_h[i * 512:(i + 1) * 512, :]
                               .rearrange("(n p) d -> p n d", p=128)), 16)
        cy.inc(Y.dma_start(XNT[:], xn1_h[:, :].rearrange("(k p) t -> p k t", p=128)), 16)
        cy.inc(Y.dma_start(MSK[:], msk_h[:, :]), 16)
        cy.inc(Y.dma_start(IDF[:], idb_h[:, :]), 16)
        cy.inc(Y.dma_start(ONR[:], onesr_h[:, :]), 16)
        cy.inc(Y.dma_start(ON4[:], ones4_h[:, :]), 16)
        t_loads = cy.n
        cg.inc(G.dma_start(VMB[:], vmask_h[:, :]), 16)
        t_vmb = cg.n
        _w(V, (cy, t_loads))
        cv.inc(V.tensor_copy(ONB[:], ONR[:, :]))
        t_ones = cv.n

        nchunks_max = 8
        for c in range(nchunks_max):
            _w(Y, (ct, ct.n))
            cy.inc(Y.dma_start(KC[c % 2][:],
                               ktf_h[:, 128 * 128 + c * 1024:128 * 128 + (c + 1) * 1024]
                               .rearrange("(k p) t -> p k t", p=128)), 16)
            t_kc = cy.n
            for half in range(2):
                pbap = PLL[:] if half == 0 else P[2][0:1, 0:512]
                for k in range(ND):
                    _w(V, (cy, t_kc), (ct, ct.n), (cv, cv.n))
                    sl = KC[c % 2][:, k * 1024 + half * 512:k * 1024 + (half + 1) * 512]
                    cv.inc(V.tensor_mul(SQ[:], sl, sl))
                    _w(T, (cv, cv.n))
                    ct.inc(T.matmul(pbap, ON4[:, 0:1], SQ[:],
                                    start=(k == 0), stop=(k == ND - 1),
                                    skip_group_check=True))
            _w(V, (ct, ct.n))
            cv.inc(V.tensor_copy(MKR[0:1, 0:512], PLL[:]))
            _w(V, (cv, cv.n))
            cv.inc(V.tensor_tensor(MKR[0:1, 0:512], MKR[0:1, 0:512],
                                   P[2][0:1, 0:512], ALU.max))
            _w(V, (cv, cv.n))
            cv.inc(V.reduce_max(RSC[0:1, 0:1], MKR[0:1, 0:512], axis=AX.X))
            _w(V, (cv, cv.n))
            if c == 0:
                cv.inc(V.tensor_copy(RSC[0:1, 1:2], RSC[0:1, 0:1]))
            else:
                cv.inc(V.tensor_tensor(RSC[0:1, 1:2], RSC[0:1, 1:2], RSC[0:1, 0:1],
                                       ALU.max))
        _w(S, (cv, cv.n))
        cs.inc(S.activation(RSC[0:1, 2:3], RSC[0:1, 1:2], AF.Sqrt))
        t_mk = cs.n

        _w(T, (cg, t_vmb))
        yfree = {}
        stfree = {}
        for g in range(4):
            nkb = NKB_FIX[g]
            nch = nkb // 8
            segc = SEG_OFF[g] * 128
            _w(Y, (ct, ct.n))
            cy.inc(Y.dma_start(QSG[:],
                               qs_h[:, g * 512:(g + 1) * 512]
                               .rearrange("(k p) t -> p k t", p=128)), 16)
            t_qs = cy.n
            for k in range(ND):
                _w(V, (cy, t_qs), (ct, ct.n), (cv, cv.n))
                sl = QSG[:, k * 512:(k + 1) * 512]
                cv.inc(V.tensor_mul(SQ[:], sl, sl))
                _w(T, (cv, cv.n))
                ct.inc(T.matmul(PLL[:], ON4[:, 0:1], SQ[:],
                                start=(k == 0), stop=(k == ND - 1),
                                skip_group_check=True))
            _w(V, (ct, ct.n))
            cv.inc(V.tensor_copy(NQR[0:1, 0:512], PLL[:]))
            _w(S, (cv, cv.n), (cs, t_mk))
            cs.inc(S.activation(NQR[0:1, 0:512], NQR[0:1, 0:512], AF.Sqrt))
            _w(V, (cs, cs.n))
            cv.inc(V.tensor_scalar(NEGC[0:1, 0:512], NQR[0:1, 0:512],
                                   RSC[0:1, 2:3], -1.01, ALU.mult, ALU.mult))
            _w(V, (cv, cv.n))
            t_negc = cv.inc(V.tensor_copy(NEGCB[:], NEGC[0:1, 0:512]))
            y_evac_tok = cv.n
            pt_done = {}
            ymm_done = {}
            t_g0 = ct.n
            for c in range(nch):
                _w(Y, (ct, ymm_done.get(c - 2, t_g0)))
                cy.inc(Y.dma_start(KC[c % 2][:],
                                   ktf_h[:, segc + c * 1024:segc + (c + 1) * 1024]
                                   .rearrange("(k p) t -> p k t", p=128)), 16)
                cy.inc(Y.dma_start(VC[c % 2][:],
                                   vf_h[SEG_OFF[g] * 128 + c * 1024:
                                        SEG_OFF[g] * 128 + (c + 1) * 1024, :]
                                   .rearrange("(k p) d -> p k d", p=128)), 16)
                t_cld = cy.n
                kbs = list(range(8))
                for kb in kbs:
                    kbg = 8 * c + kb
                    pb = P[kb % 2]
                    _w(T, (cs, stfree.get(kb % 2, 0)), (cy, t_cld), (cv, t_negc))
                    for k in range(ND):
                        T.matmul(pb[:], KC[c % 2][:, k * 1024 + kb * 128:
                                                  k * 1024 + (kb + 1) * 128],
                                 QSG[:, k * 512:(k + 1) * 512],
                                 start=(k == 0), stop=False)
                    ct.inc(T.matmul(pb[:], ONB[:, :], NEGCB[0:1, :],
                                    start=False, stop=True))
                    st_tok = ct.n
                    dst = PTS[:, kb * 512:(kb + 1) * 512]
                    if kbg >= nkb - 4:
                        m = kbg - (nkb - 4)
                        _w(V, (ct, st_tok), (cs, cs.n))
                        cv.inc(V.tensor_tensor(TMS[:], pb[:], MSK[:, m * 512:
                                                                  (m + 1) * 512],
                                               ALU.add))
                        _w(S, (cv, cv.n), (ct, ymm_done.get(c - 1, 0)))
                        cs.inc(S.activation(dst, TMS[:], AF.Exp))
                    else:
                        _w(S, (ct, st_tok), (ct, ymm_done.get(c - 1, 0)))
                        cs.inc(S.activation(dst, pb[:], AF.Exp))
                    stfree[kb % 2] = cs.n
                    pt_done[kbg] = cs.n
                    _w(T, (cs, pt_done[kbg]))
                    for sub in range(4):
                        inst = T.matmul(PLC[:, sub:sub + 1],
                                        PTS[:, kb * 512 + sub * 128:
                                            kb * 512 + (sub + 1) * 128],
                                        VMB[:, SEG_OFF[g] + kbg:SEG_OFF[g] + kbg + 1],
                                        start=(kbg == 0 and sub == 0),
                                        stop=(kbg == nkb - 1 and sub == 3),
                                        skip_group_check=True)
                    ct.inc(inst)
                for sub in range(4):
                    for dh in range(2):
                        nb = 2 + (sub * 2 + dh) % 2
                        pb = P[nb]
                        _w(T, (cv, yfree.get(nb, 0)))
                        for n_i, kb in enumerate(kbs):
                            inst = T.matmul(pb[:], PTS[:, kb * 512 + sub * 128:
                                              kb * 512 + (sub + 1) * 128],
                                     VC[c % 2][:, kb * 1024 + dh * 512:
                                               kb * 1024 + (dh + 1) * 512],
                                     start=(n_i == 0), stop=(n_i == len(kbs) - 1))
                        ct.inc(inst)
                        _w(V, (ct, ct.n))
                        dsty = YP[:, sub * 1024 + dh * 512:sub * 1024 + (dh + 1) * 512]
                        if c == 0:
                            yfree[nb] = cv.inc(V.tensor_copy(dsty, pb[:]))
                        else:
                            yfree[nb] = cv.inc(V.tensor_tensor(dsty, dsty, pb[:],
                                                               ALU.add))
                        y_evac_tok = cv.n
                ymm_done[c] = ct.n
            _w(V, (ct, ct.n))
            for sub in range(4):
                cv.inc(V.reciprocal(RI[:, sub:sub + 1], PLC[:, sub:sub + 1]))
            for sub in range(4):
                _w(V, (cv, cv.n))
                tt = tile_base[g] + sub
                sl = ACC[:, tt * D:(tt + 1) * D]
                cv.inc(V.scalar_tensor_tensor(sl, YP[:, sub * 1024:(sub + 1) * 1024],
                                              RI[:, sub:sub + 1], sl,
                                              ALU.mult, ALU.add))
        t_attn_done_v = cv.n
        t_attn_done_t = ct.n
        t_attn_done_s = cs.n
        if attn_only:
            _w(G, (cv, cv.n))
            co.inc(G.dma_start(dbg_negc[:, :], NEGC), 16)
            co.inc(G.dma_start(dbg_lr[:, :], RI[0:1, 0:4]), 16)
            co.inc(G.dma_start(dbg_yp[:, :], YP), 16)
            co.inc(G.dma_start(dbg_pts[:, :], PTS), 16)
            for i in range(4):
                co.inc(G.dma_start(out_h[i * 512:(i + 1) * 512, :]
                                   .rearrange("(n p) d -> p n d", p=128),
                                   ACC[:, i * 4 * D:(i + 1) * 4 * D]), 16)
            G.wait_ge(co.h, co.n)
            return nc

        cg.inc(G.dma_start(G1W[:], g1_h[:, :].rearrange("(k p) e -> p k e", p=128)), 16)
        t_g1 = cg.n
        for i in range(NTT):
            _w(T, (cg, t_g1), (cs, cs.n), (cv, cv.n))
            for k in range(ND):
                inst = T.matmul(P[0][:, 0:15],
                                XNT[:, k * T_CORE + i * 128:k * T_CORE + (i + 1) * 128],
                                G1W[:, k * 15:(k + 1) * 15],
                                start=(k == 0), stop=(k == ND - 1))
            ct.inc(inst)
            _w(V, (ct, ct.n))
            cv.inc(V.reduce_max(RSC[:, 0:1], P[0][:, 0:15], axis=AX.X, negate=True))
            _w(S, (cv, cv.n))
            cs.inc(S.activation(ESC[:], P[0][:, 0:15], AF.Exp, bias=RSC[:, 0:1]))
            _w(V, (cs, cs.n))
            cv.inc(V.tensor_copy(SELB[:], ESC[:]))
            for it in range(3):
                _w(V, (cv, cv.n))
                cv.inc(V.reduce_max(RSC[:, 1:2], SELB[:], axis=AX.X))
                if it < 2:
                    _w(V, (cv, cv.n))
                    cv.inc(V.tensor_scalar(TMP15[:], SELB[:], RSC[:, 1:2], None,
                                           ALU.is_equal))
                    _w(V, (cv, cv.n))
                    cv.inc(V.scalar_tensor_tensor(SELB[:], TMP15[:], -BIG, SELB[:],
                                                  ALU.mult, ALU.add))
            _w(V, (cv, cv.n))
            cv.inc(V.tensor_scalar(TMP15[:], ESC[:], RSC[:, 1:2], None, ALU.is_ge))
            _w(V, (cv, cv.n))
            cv.inc(V.tensor_mul(TMP15[:], TMP15[:], ESC[:]))
            _w(V, (cv, cv.n))
            cv.inc(V.reduce_sum(RSC[:, 2:3], TMP15[:], axis=AX.X))
            _w(V, (cv, cv.n))
            cv.inc(V.tensor_scalar_add(RSC[:, 2:3], RSC[:, 2:3], 1e-20))
            _w(V, (cv, cv.n))
            cv.inc(V.reciprocal(RSC[:, 3:4], RSC[:, 2:3]))
            _w(V, (cv, cv.n))
            cv.inc(V.tensor_scalar_mul(COMB1[:, i * 16:i * 16 + 15], TMP15[:],
                                       RSC[:, 3:4]))
            cv.inc(V.memset(COMB1[:, i * 16 + 15:i * 16 + 16], 1.0))
            _w(T, (cv, cv.n))
            ct.inc(T.transpose(PCT[:], COMB1[:, i * 16:(i + 1) * 16], IDF[:]))
            _w(V, (ct, ct.n))
            cv.inc(V.tensor_copy(AR4[0:16, 2048 + i * 128:2048 + (i + 1) * 128],
                                 PCT[:]))
        t_comb1 = cv.n
        CT = AR4[0:16, 2048:4096]
        FB = AR4[0:16, 4096:5120]
        HP = AR4[:, 0:1024]
        GL = AR4[:, 1024:1536]
        FW = AR1[:, 0:16384]
        FC2 = AR2[:, 0:8192]
        AB = AR2[:, 8192:12288]
        SF2 = AR1[:, 0:16384]
        SAB = AR2[:, 0:8192]
        SW = [AR2[:, 8192 + i * 2048:8192 + (i + 1) * 2048] for i in range(2)]

        _w(Y, (cv, t_attn_done_v), (cs, t_attn_done_s))
        cy.inc(Y.dma_start(AR4[0:16, 4096:5120], fba_h[:, :]), 16)
        t_fb = cy.n
        for i in range(NTT):
            for dh in range(2):
                pb = P[1 + (2 * i + dh) % 2]
                _w(T, (cv, t_comb1), (cy, t_fb), (cv, cv.n))
                ct.inc(T.matmul(pb[:], CT[:, i * 128:(i + 1) * 128],
                                FB[:, dh * 512:(dh + 1) * 512],
                                start=True, stop=True))
                _w(V, (ct, ct.n))
                sl = ACC[:, i * D + dh * 512:i * D + (dh + 1) * 512]
                cv.inc(V.tensor_tensor(sl, sl, pb[:], ALU.add))

        ffs = {nb: cs.n for nb in range(4)}
        ffv = {nb: cv.n for nb in range(4)}
        fc2_done = {}
        fc1_done = {}
        fc2exp_done = {}
        for e in range(15):
            _w(G, (ct, fc1_done.get(e - 1, ct.n)))
            cg.inc(G.dma_start(FW, f1w_h[e, :, :]
                               .rearrange("(k p) f -> p k f", p=128)), 16)
            t_wl = cg.n
            _w(G, (ct, fc2exp_done.get(e - 1, ct.n)))
            cg.inc(G.dma_start(FC2, f2w_h[e, :, :]
                               .rearrange("(k p) d -> p k d", p=128)), 16)
            _w(Y, (cs, cs.n))
            cy.inc(Y.dma_start(F1B[:], f1b_h[e, :, :]), 16)
            cy.inc(Y.dma_start(GEG[:], geg_h[e, :, :]), 16)
            t_fc2l, t_bl = cg.n, cy.n
            for tc in range(4):
                t_abw = fc2_done.get(tc - 1, 0)
                for fp in range(8):
                    for half in range(2):
                        nb = (fp * 2 + half) % 2
                        pb = P[nb]
                        fti = fp + 8 * half
                        _w(T, (cg, t_wl), (cs, ffs.get(nb, 0)))
                        for k in range(ND):
                            inst = T.matmul(pb[:],
                                            FW[:, k * 2048 + fti * 128:
                                               k * 2048 + (fti + 1) * 128],
                                            XNT[:, k * T_CORE + tc * 512:
                                                k * T_CORE + (tc + 1) * 512],
                                            start=(k == 0), stop=(k == ND - 1))
                        ct.inc(inst)
                        _w(S, (ct, ct.n), (cy, t_bl))
                        ffs[nb] = cs.inc(S.activation(
                            HP[:, half * 512:(half + 1) * 512], pb[:],
                            AF.Identity, bias=F1B[:, fti:fti + 1]))
                    _w(S, (cs, cs.n))
                    cs.inc(S.activation(GL[:], HP[:, 512:1024], AF.Gelu))
                    _w(V, (cs, cs.n), (ct, t_abw))
                    cv.inc(V.scalar_tensor_tensor(AB[:, fp * 512:(fp + 1) * 512],
                                                  GL[:], GEG[:, fp:fp + 1],
                                                  HP[:, 0:512], ALU.mult, ALU.mult))
                t_a = cv.n
                if tc == 3:
                    fc1_done[e] = ct.n
                _w(T, (cv, t_a), (cg, t_fc2l))
                for ts4 in range(4):
                    for dh in range(2):
                        nb = 2 + (ts4 * 2 + dh) % 2
                        pb = P[nb]
                        _w(T, (cv, ffv.get(nb, 0)))
                        for ft in range(8):
                            inst = T.matmul(pb[:],
                                            AB[:, ft * 512 + ts4 * 128:
                                               ft * 512 + (ts4 + 1) * 128],
                                            FC2[:, ft * D + dh * 512:
                                                ft * D + (dh + 1) * 512],
                                            start=(ft == 0), stop=(ft == 7))
                        ct.inc(inst)
                        _w(V, (ct, ct.n))
                        i = tc * 4 + ts4
                        sl = ACC[:, i * D + dh * 512:i * D + (dh + 1) * 512]
                        ffv[nb] = cv.inc(V.scalar_tensor_tensor(
                            sl, pb[:], COMB1[:, i * 16 + e:i * 16 + e + 1], sl,
                            ALU.mult, ALU.add))
                fc2_done[tc] = ct.n
            fc2exp_done[e] = ct.n

        _w(G, (ct, ct.n))
        cg.inc(G.dma_start(SF2, s2w_h[:, :].rearrange("(k p) d -> p k d", p=128)), 16)
        cy.inc(Y.dma_start(S1B[:], s1b_h[:, :]), 16)
        cy.inc(Y.dma_start(SGG[:], sgg_h[:, :]), 16)
        t_sw, t_sb = cg.n, cy.n
        sfc2_done = {}
        sw_done = {}
        for tc in range(4):
            t_abw = sfc2_done.get(tc - 1, 0)
            for i in range(16):
                gp = tc * 16 + i
                _w(G, (ct, sw_done.get(gp - 2, ct.n if gp == 0 else 0)))
                cg.inc(G.dma_start(SW[i % 2], s1w_h[i, :, :]
                                   .rearrange("(k p) f -> p k f", p=128)), 16)
                t_wl = cg.n
                for half in range(2):
                    nb = (i * 2 + half) % 2
                    pb = P[nb]
                    _w(T, (cg, t_wl), (cs, ffs.get(nb, 0)))
                    for k in range(ND):
                        inst = T.matmul(pb[:],
                                        SW[i % 2][:, k * 256 + half * 128:
                                                  k * 256 + (half + 1) * 128],
                                        XNT[:, k * T_CORE + tc * 512:
                                            k * T_CORE + (tc + 1) * 512],
                                        start=(k == 0), stop=(k == ND - 1))
                    ct.inc(inst)
                    _w(S, (ct, ct.n), (cy, t_sb))
                    ffs[nb] = cs.inc(S.activation(
                        HP[:, half * 512:(half + 1) * 512], pb[:],
                        AF.Identity, bias=S1B[:, 16 * half + i:16 * half + i + 1]))
                _w(S, (cs, cs.n))
                cs.inc(S.activation(GL[:], HP[:, 512:1024], AF.Gelu))
                _w(V, (cs, cs.n), (ct, t_abw))
                cv.inc(V.scalar_tensor_tensor(SAB[:, i * 512:(i + 1) * 512],
                                              GL[:], SGG[:, i:i + 1],
                                              HP[:, 0:512], ALU.mult, ALU.mult))
                sw_done[gp] = ct.n
            t_a = cv.n
            _w(T, (cv, t_a), (cg, t_sw))
            for ts4 in range(4):
                for dh in range(2):
                    nb = 2 + (ts4 * 2 + dh) % 2
                    pb = P[nb]
                    _w(T, (cv, ffv.get(nb, 0)))
                    for ft in range(16):
                        inst = T.matmul(pb[:],
                                        SAB[:, ft * 512 + ts4 * 128:
                                           ft * 512 + (ts4 + 1) * 128],
                                        SF2[:, ft * D + dh * 512:
                                            ft * D + (dh + 1) * 512],
                                        start=(ft == 0), stop=(ft == 15))
                    ct.inc(inst)
                    _w(V, (ct, ct.n))
                    i = tc * 4 + ts4
                    sl = ACC[:, i * D + dh * 512:i * D + (dh + 1) * 512]
                    ffv[nb] = cv.inc(V.tensor_tensor(sl, sl, pb[:], ALU.add))
            sfc2_done[tc] = ct.n
        t_final = cv.n

        _w(G, (cv, t_final))
        for i in range(4):
            co.inc(G.dma_start(out_h[i * 512:(i + 1) * 512, :]
                               .rearrange("(n p) d -> p n d", p=128),
                               ACC[:, i * 4 * D:(i + 1) * 4 * D]), 16)
        G.wait_ge(co.h, co.n)
    return nc



def _np(a):
    return np.ascontiguousarray(np.asarray(a), dtype=np.float32)


def kernel(**kw):
    x = _np(kw["L0_attn_query"])
    W_qkv0 = _np(kw["L0_attn_W_qkv"])
    b_qkv0 = _np(kw["L0_attn_b_qkv"])
    W_out0 = _np(kw["L0_attn_W_out"])
    b_out0 = _np(kw["L0_attn_b_out"])
    wq_w = _np(kw["L0_attn_weight_q"])
    wv_w = _np(kw["L0_attn_weight_v"])
    gate0 = _np(kw["L0_ffn_gate_w"])
    bias_e = _np(kw["L0_ffn_bias_e"])
    up_w = _np(kw["L0_ffn_up_w"])
    down_w = _np(kw["L0_ffn_down_w"])
    sh_up = _np(kw["L0_ffn_shared_up"])
    sh_dn = _np(kw["L0_ffn_shared_down"])
    W_qkv1 = _np(kw["L1_attn_W_qkv"])
    b_qkv1 = _np(kw["L1_attn_b_qkv"])
    gate1 = _np(kw["L1_ffn_gate_w"])
    fc1_w = _np(kw["L1_ffn_fc1_w"])
    fc1_b = _np(kw["L1_ffn_fc1_b"])
    geglu = _np(kw["L1_ffn_geglu_mult"])
    fc2_w = _np(kw["L1_ffn_fc2_w"])
    fc2_b = _np(kw["L1_ffn_fc2_b"])
    s_fc1_w = _np(kw["L1_ffn_s_fc1_w"])
    s_fc1_b = _np(kw["L1_ffn_s_fc1_b"])
    s_geglu = _np(kw["L1_ffn_s_geglu_mult"])
    s_fc2_w = _np(kw["L1_ffn_s_fc2_w"])
    s_fc2_b = _np(kw["L1_ffn_s_fc2_b"])

    C = np.ascontiguousarray

    def pcol(v):
        k = v.size // 128
        return C(v.reshape(k, 128).T)

    shared_A = {
        "wq0T": C(W_qkv0[:1024].T), "wv0T": C(W_qkv0[2048:].T),
        "bq0": pcol(b_qkv0[:1024]), "bv0": pcol(b_qkv0[2048:]),
        "wavq": C(wq_w.transpose(0, 2, 1).reshape(1024, 3)
                  .reshape(8, 128, 3).transpose(1, 0, 2).reshape(128, 24)),
        "wavv": C(wv_w.transpose(0, 2, 1).reshape(1024, 3)
                  .reshape(8, 128, 3).transpose(1, 0, 2).reshape(128, 24)),
        "woT": C(W_out0.T), "bo": C(b_out0[None, :]),
        "gateT": C(gate0.T), "biase": C(np.tile(bias_e[None, :], (128, 1))),
        "upT": C(up_w.transpose(0, 2, 1)), "dnT": C(down_w.transpose(0, 2, 1)),
        "supT": C(sh_up.T), "sdnT": C(sh_dn.T),
        "w1T": C(W_qkv1.T),
        "b1qk": C(np.concatenate([pcol(b_qkv1[:1024]), pcol(b_qkv1[1024:2048])], 1)),
        "b1v": C(b_qkv1[2048:][None, :]),
        "idb": np.eye(128, dtype=np.float32),
        "onesr": np.ones((1, 128), np.float32),
        "ones4": np.ones((128, 4), np.float32),
        "wsel": C(np.tile((2.0 ** (15 - np.arange(16, dtype=np.float32)))[None, :],
                          (128, 1))),
    }
    ii = np.arange(128)[:, None]
    jj = np.arange(512)[None, :]
    masks = np.concatenate(
        [np.where(ii <= jj - 128 * m, 0.0, -BIG).astype(np.float32) for m in range(4)],
        axis=1)
    sf1T = C(s_fc1_w.T)
    s1P = C(np.concatenate([sf1T[:, :2048].reshape(1024, 16, 128),
                            sf1T[:, 2048:].reshape(1024, 16, 128)], axis=2)
            .transpose(1, 0, 2))
    shared_B = {
        "g1T": C(gate1.T),
        "f1T": C(fc1_w.transpose(0, 2, 1)),
        "f1b": C(fc1_b.reshape(15, 16, 128).transpose(0, 2, 1)),
        "geg": C(geglu.reshape(15, 8, 128).transpose(0, 2, 1)),
        "f2T": C(fc2_w.transpose(0, 2, 1)),
        "fbaug": C(np.concatenate([fc2_b, s_fc2_b[None, :]], 0)),
        "s1P": s1P,
        "s1b": C(s_fc1_b.reshape(32, 128).T),
        "sgeg": C(s_geglu.reshape(16, 128).T),
        "s2T": C(s_fc2_w.T),
        "masks": masks,
        "idb": np.eye(128, dtype=np.float32),
        "onesr": np.ones((1, 128), np.float32),
        "ones4": np.ones((128, 4), np.float32),
    }

    ncA = build_A()
    in_maps = []
    for core in range(8):
        b, j = core // 4, core % 4
        xp = np.concatenate([x[b, j * 1024:(j + 1) * 1024],
                             x[b, (7 - j) * 1024:(8 - j) * 1024]], axis=0)
        m = dict(shared_A)
        m["x"] = C(xp)
        in_maps.append(m)
    resA = run_bass_kernel_spmd(ncA, in_maps, core_ids=list(range(8)))
    RA = resA.results

    import ml_dtypes
    ktF = np.zeros((2, D, 8192), dtype=ml_dtypes.bfloat16)
    vF = np.zeros((2, 8192, D), dtype=ml_dtypes.bfloat16)
    for core in range(8):
        b, j = core // 4, core % 4
        kt = RA[core]["kT"]
        vv = RA[core]["vN"]
        ktF[b][:, j * 1024:(j + 1) * 1024] = kt[:, :1024]
        ktF[b][:, (7 - j) * 1024:(8 - j) * 1024] = kt[:, 1024:]
        vF[b][j * 1024:(j + 1) * 1024] = vv[:1024]
        vF[b][(7 - j) * 1024:(8 - j) * 1024] = vv[1024:]

    ncB = build_B()
    out = np.empty((2, 8192, D), dtype=np.float32)
    mapsB = []
    NKB_TRUE = lambda j, g: (8 * j + 4, 8 * j + 8, 8 * (7 - j) + 4,
                             8 * (7 - j) + 8)[g]
    for core in range(8):
        b, j = core // 4, core % 4
        ktG = np.zeros((D, 192 * 128), dtype=ml_dtypes.bfloat16)
        vG = np.zeros((192 * 128, D), dtype=ml_dtypes.bfloat16)
        vmask = np.zeros((128, 192), dtype=np.float32)
        for g in range(4):
            nfix = NKB_FIX[g]
            ntrue = NKB_TRUE(j, g)
            pad = nfix - ntrue
            c0 = SEG_OFF[g] * 128
            ktG[:, c0 + pad * 128:c0 + nfix * 128] = ktF[b][:, :ntrue * 128]
            vG[c0 + pad * 128:c0 + nfix * 128, :] = vF[b][:ntrue * 128, :]
            vmask[:, SEG_OFF[g] + pad:SEG_OFF[g] + nfix] = 1.0
        m = dict(shared_B)
        m["out1"] = RA[core]["out1"]
        m["xn1T"] = RA[core]["xn1T"]
        m["qsT"] = RA[core]["qsT"]
        m["ktG"] = ktG
        m["vG"] = vG
        m["vmask"] = vmask
        mapsB.append(m)
    resB = run_bass_kernel_spmd(ncB, mapsB, core_ids=list(range(8)))
    for core in range(8):
        b, j = core // 4, core % 4
        o = resB.results[core]["out"]
        out[b, j * 1024:(j + 1) * 1024] = o[:1024]
        out[b, (7 - j) * 1024:(8 - j) * 1024] = o[1024:]
    return out
